# version 01
# baseline (speedup 1.0000x reference)
import contextlib
import numpy as np
import ml_dtypes

import concourse.bass as bass
import concourse.mybir as mybir
from concourse.bass_utils import run_bass_kernel_spmd

F32 = mybir.dt.float32
BF16 = mybir.dt.bfloat16
AF = mybir.ActivationFunctionType
ALU = mybir.AluOpType
AX = mybir.AxisListType

D = 1024
S = 4096
B = 4
T = 2048
NG = 8
NQB = 4
NKC = 32
NT = 16
NCHUNK = 32
FI = 2816
QUANT_THR = [0.25, 0.75, 1.25, 1.75, 2.5, 3.5, 5.0]
QUANT_W2 = [1.0, 1.0, 1.0, 1.0, 2.0, 2.0, 4.0]
FFN_SLABS = [512, 512, 512, 512, 512, 256]
TRUNC = None
NOWQ = False
NOX = False
DELAY_WQ = False
DEBUG_DUMP = False
LN06 = float(np.float32(np.log(0.6)))
INV06 = float(np.float32(1.0))


class Ctr:
    def __init__(self, es, nc, name):
        self.h = es.enter_context(nc.semaphore(name))
        self.n = 0


class Tag:

    def __init__(self, lst):
        self.lst = lst
        self.idx = len(lst) - 1

    def then_inc(self, h, k):
        f = self.lst[self.idx]

        def g(e, f=f, h=h, k=k):
            r = f(e)
            r.then_inc(h, k)
            return r

        self.lst[self.idx] = g


def inc(stream, ctr, k=1):
    Tag(stream).then_inc(ctr.h, k)
    ctr.n += k
    return ctr.n


def wt(stream, ctr, val=None):
    v = ctr.n if val is None else val
    if v > 0:
        stream.append(lambda e, h=ctr.h, vv=v: e.wait_ge(h, vv))


def build_lband_consts(half):
    w = np.array([np.exp(-(d * d) / (2.0 * 0.4 ** 2)) for d in range(-3, 4)])

    def l04(qo, jo):
        if jo < 0 or jo >= S or abs(qo - jo) > 3:
            return 0.0
        z = sum(w[dd + 3] for dd in range(-3, 4) if 0 <= qo + dd < S)
        return 0.4 * w[(jo - qo) + 3] / (z + 1e-10)

    q0 = half * T

    def orig(r):
        return (q0 + r) % S

    def band(dlt):
        if abs(dlt) > 3:
            return 0.0
        return 0.4 * w[-dlt + 3] / (w.sum() + 1e-10)

    pp, uu = np.meshgrid(np.arange(128), np.arange(896), indexing="ij")
    lsh = np.vectorize(band)(uu - 384 - pp)

    def main_tile(qb, jl):
        t = np.zeros((128, 512))
        for p in range(128):
            jr = (qb * 4 + jl) * 128 + p
            lo = max(0, p + 128 * jl * 0 + (jl * 128 + 0) * 0)
            for c in range(max(0, jl * 128 + p - 3), min(512, jl * 128 + p + 4)):
                t[p, c] = l04(orig(qb * 512 + c), orig(jr))
        return t

    lm_first = main_tile(0, 0)
    lm_last = main_tile(3, 3)

    def fringe(qb, pre):
        t = np.zeros((128, 128))
        if pre:
            jc = (qb * 4 - 1) % NCHUNK
            for p in range(125, 128):
                for c in range(0, 3):
                    t[p, c] = l04(orig(qb * 512 + c), orig(jc * 128 + p))
        else:
            jc = (qb * 4 + 4) % NCHUNK
            for p in range(0, 3):
                for c in range(125, 128):
                    t[p, c] = l04(orig(qb * 512 + 384 + c), orig(jc * 128 + p))
        return t

    out = {
        "lsh": lsh,
        "lm_first": lm_first, "lm_last": lm_last,
        "lpre0": fringe(0, True), "lpost3": fringe(3, False),
        "lpre_sh": fringe(1, True), "lpost_sh": fringe(1, False),
    }
    return {k: np.ascontiguousarray(v, np.float32) for k, v in out.items()}


def build_program():
    nc = bass.Bass("TRN2", target_bir_lowering=False, debug=False,
                   detect_race_conditions=False)

    d_x = nc.dram_tensor("xb", [S, D], F32, kind="ExternalInput").ap()
    d_w = {}
    for nm in ["wqT", "wkT", "wvT", "woT"]:
        d_w[nm] = nc.dram_tensor(nm, [D, D], F32, kind="ExternalInput").ap()
    d_b = {}
    for nm in ["bq", "bk", "bv", "bo"]:
        d_b[nm] = nc.dram_tensor(nm, [D], F32, kind="ExternalInput").ap()
    d_wg = nc.dram_tensor("wg", [FI, D], F32, kind="ExternalInput").ap()
    d_wu = nc.dram_tensor("wu", [FI, D], F32, kind="ExternalInput").ap()
    d_wd = nc.dram_tensor("wd", [D, FI], F32, kind="ExternalInput").ap()
    d_ident = nc.dram_tensor("ident", [128, 128], BF16, kind="ExternalInput").ap()
    lshapes = [("lsh", 896), ("lm_first", 512), ("lm_last", 512),
               ("lpre0", 128), ("lpost3", 128), ("lpre_sh", 128),
               ("lpost_sh", 128)]
    d_l = {nm: nc.dram_tensor("L_" + nm, [128, w], F32,
                              kind="ExternalInput").ap()
           for nm, w in lshapes}
    d_out = nc.dram_tensor("out", [T, D], F32, kind="ExternalOutput").ap()
    if DEBUG_DUMP == 2:
        d_dbg_kt = nc.dram_tensor("dbg_kt", [128, 512], BF16,
                                  kind="ExternalOutput").ap()
        d_dbg_qt = nc.dram_tensor("dbg_qt", [128, 512], BF16,
                                  kind="ExternalOutput").ap()
    if DEBUG_DUMP:
        d_dbg_ut = nc.dram_tensor("dbg_ut", [128, 8 * T], BF16,
                                  kind="ExternalOutput").ap()
        d_dbg_acc = nc.dram_tensor("dbg_acc", [128, NT * D], F32,
                                   kind="ExternalOutput").ap()
        d_dbg_xqt = nc.dram_tensor("dbg_xqt", [128, 8 * T], BF16,
                                   kind="ExternalOutput").ap()
        d_dbg_wgq = nc.dram_tensor("dbg_wgq", [FI, D], BF16,
                                   kind="ExternalOutput").ap()
        d_dbg_wgt = nc.dram_tensor("dbg_wgt", [128, 8 * 512], BF16,
                                   kind="ExternalOutput").ap()
        d_dbg_wdt = nc.dram_tensor("dbg_wdt", [128, 4 * 1024], BF16,
                                   kind="ExternalOutput").ap()
    d_sgq = nc.dram_tensor("scr_wgq", [FI, D], BF16, kind="Internal").ap()
    d_suq = nc.dram_tensor("scr_wuq", [FI, D], BF16, kind="Internal").ap()
    d_sdq = nc.dram_tensor("scr_wdq", [D, FI], BF16, kind="Internal").ap()

    es = contextlib.ExitStack()

    PE, SA, DV, SP, GP = [], [], [], [], []

    c_dx = [Ctr(es, nc, "dx0"), Ctr(es, nc, "dx1")]
    c_dxf = [Ctr(es, nc, "dxf0"), Ctr(es, nc, "dxf1")]
    c_dw = Ctr(es, nc, "dw")
    c_dwg = [Ctr(es, nc, "dwg0"), Ctr(es, nc, "dwg1")]
    c_dbo = Ctr(es, nc, "dbo")
    c_dbg = Ctr(es, nc, "dbg")
    c_dwo = Ctr(es, nc, "dwo")
    c_dfw = [Ctr(es, nc, "dfw0"), Ctr(es, nc, "dfw1")]
    c_dsw = [Ctr(es, nc, "dsw0"), Ctr(es, nc, "dsw1")]
    c_dst = [Ctr(es, nc, "dst0"), Ctr(es, nc, "dst1")]
    c_dout = Ctr(es, nc, "dout")
    c_reloc = [Ctr(es, nc, "creloc0"), Ctr(es, nc, "creloc1")]
    c_pe = Ctr(es, nc, "cpe")
    c_act = Ctr(es, nc, "cact")
    c_dve = Ctr(es, nc, "cdve")
    c_exp = Ctr(es, nc, "cexp")

    id_bf = es.enter_context(nc.sbuf_tensor("id_bf", [128, 128], BF16))
    ones_all = es.enter_context(nc.sbuf_tensor("ones_all", [128, 128], BF16))
    b_qkv = es.enter_context(nc.sbuf_tensor("b_qkv", [128, 24], F32))
    bo_row = es.enter_context(nc.sbuf_tensor("bo_row", [1, D], BF16))
    r6 = es.enter_context(nc.sbuf_tensor("r6", [128, 1], F32))
    lc_ln06 = es.enter_context(nc.sbuf_tensor("lc_ln06", [128, 1], F32))
    lc_eps = es.enter_context(nc.sbuf_tensor("lc_eps", [128, 1], F32))
    lbf = {nm: es.enter_context(nc.sbuf_tensor("lbf_" + nm, [128, w], BF16))
           for nm, w in lshapes}
    q_ax = es.enter_context(nc.sbuf_tensor("q_ax", [128, 1024], F32))
    q_sg = es.enter_context(nc.sbuf_tensor("q_sg", [128, 1024], F32))
    q_c = [es.enter_context(nc.sbuf_tensor(f"q_c{i}", [128, 1024], BF16))
           for i in range(3)]
    q_bm = es.enter_context(nc.sbuf_tensor("q_bm", [128, 64], F32))
    q_s = es.enter_context(nc.sbuf_tensor("q_s", [128, 64], F32))
    q_rs = es.enter_context(nc.sbuf_tensor("q_rs", [128, 64], F32))
    q_sh = es.enter_context(nc.sbuf_tensor("q_sh", [128, 64], F32))

    ps_tr = es.enter_context(nc.psum_tensor("ps_tr", [128, 1024], BF16))
    ut_scope = es.enter_context(contextlib.ExitStack())
    ut = ut_scope.enter_context(nc.sbuf_tensor("ut", [128, 8 * T], BF16))
    wfq_in = [ut_scope.enter_context(nc.sbuf_tensor(f"wfqi{i}", [128, 1024],
                                                    F32)) for i in range(2)]
    wfq_out = [ut_scope.enter_context(nc.sbuf_tensor(f"wfqo{i}", [128, 1024],
                                                     BF16)) for i in range(2)]

    def bc16(t, nblk):
        return bass.AP(t, 0, [[t.shape[1], 128], [1, nblk], [0, 16]])

    def r3d(ap_):
        return ap_.rearrange("p (b s) -> p b s", s=16)

    def emit_quant(src_ap, dst_ap, fd):
        nb = fd // 16
        ax = q_ax[:, 0:fd]
        sg = q_sg[:, 0:fd]
        bm = q_bm[:, 0:nb]
        cb = [q_c[i][:, 0:fd] for i in range(3)]
        DV.append(lambda v, a=src_ap, o=bm: v.tensor_reduce(
            o, a.rearrange("p (b s) -> p b s", s=16), axis=AX.X, op=ALU.max,
            apply_absolute_value=True))
        DV.append(lambda v: v.tensor_scalar(
            q_s[:, 0:64], q_bm[:, 0:64], r6[:, 0:1], 1e-12, ALU.mult, ALU.max))
        DV.append(lambda v: v.reciprocal(q_rs[:, 0:64], q_s[:, 0:64]))
        DV.append(lambda v: v.tensor_scalar(
            q_sh[:, 0:64], q_s[:, 0:64], 0.5, None, ALU.mult))
        DV.append(lambda v, i=src_ap, o=ax: v.tensor_scalar(
            o, i, -1.0, None, ALU.mult))
        DV.append(lambda v, i=src_ap, o=ax: v.tensor_tensor(o, o, i, ALU.max))
        DV.append(lambda v, i=src_ap, o=sg: v.tensor_scalar(
            o, i, 0.0, -2.0, ALU.is_lt, ALU.mult))
        DV.append(lambda v, o=sg: v.tensor_scalar(o, o, 1.0, None, ALU.add))
        DV.append(lambda v, o=ax, n=nb: v.tensor_tensor(
            r3d(o), r3d(o), bc16(q_rs, n), ALU.mult))
        DV.append(lambda v, i=ax, o=cb[0]: v.tensor_scalar(
            o, i, QUANT_THR[0], QUANT_W2[0], ALU.is_gt, ALU.mult))
        DV.append(lambda v, i=ax, o=cb[1]: v.tensor_scalar(
            o, i, QUANT_THR[1], QUANT_W2[1], ALU.is_gt, ALU.mult))
        DV.append(lambda v, a=cb[0], b=cb[1]: v.tensor_tensor(a, a, b, ALU.add))
        for k in (2, 3, 4, 5, 6):
            DV.append(lambda v, i=ax, o=cb[1], k=k: v.tensor_scalar(
                o, i, QUANT_THR[k], QUANT_W2[k], ALU.is_gt, ALU.mult))
            DV.append(lambda v, a=cb[0], b=cb[1]: v.tensor_tensor(
                a, a, b, ALU.add))
        DV.append(lambda v, g=cb[0], o=ax, n=nb: v.tensor_tensor(
            r3d(o), r3d(g), bc16(q_sh, n), ALU.mult))
        DV.append(lambda v, a=ax, b=sg, o=dst_ap: v.tensor_tensor(
            o, a, b, ALU.mult))
        return inc(DV, c_dve)

    ph_a = es.enter_context(contextlib.ExitStack())
    xt = ph_a.enter_context(nc.sbuf_tensor("xt", [128, 8 * S], BF16))
    ph_0 = es.enter_context(contextlib.ExitStack())
    xld = [ph_0.enter_context(nc.sbuf_tensor(f"xld{i}", [128, D], BF16))
           for i in range(2)]
    l32 = {nm: ph_0.enter_context(nc.sbuf_tensor("l32_" + nm, [128, w], F32))
           for nm, w in lshapes}

    SP.append(lambda e: e.dma_start(id_bf[:], d_ident))
    inc(SP, c_dw, 16)
    for nm, _w in lshapes:
        SP.append(lambda e, a=l32[nm], b=d_l[nm]: e.dma_start(a[:], b))
        inc(SP, c_dw, 16)
    for j, nm in enumerate(["bq", "bk", "bv"]):
        SP.append(lambda e, j=j, dt=d_b[nm]: e.dma_start(
            b_qkv[:, j * 8:(j + 1) * 8],
            bass.AP(dt.tensor, 0, [[1, 128], [128, 8]]),
            allow_slow_non_contiguous=True))
        inc(SP, c_dw, 16)
    GP.append(lambda e: e.dma_start(
        bo_row[:], bass.AP(d_b["bo"].tensor, 0, [[D, 1], [1, D]])))
    inc(GP, c_dbo, 16)

    wt(DV, c_dw)
    DV.append(lambda v: v.memset(r6[:], 6.0))
    DV.append(lambda v: v.memset(q_s[:], 1.0))
    DV.append(lambda v: v.memset(q_bm[:], 1.0))
    DV.append(lambda v: v.reciprocal(r6[:], r6[:]))
    DV.append(lambda v: v.memset(ones_all[:], 1.0))
    DV.append(lambda v: v.memset(lc_ln06[:], LN06))
    DV.append(lambda v: v.memset(lc_eps[:], 1e-5))
    DV.append(lambda v: v.tensor_scalar(b_qkv[:, 0:8], b_qkv[:, 0:8], 0.125,
                                        None, ALU.mult))
    for nm, _w in lshapes:
        DV.append(lambda v, a=lbf[nm], b=l32[nm]: v.tensor_copy(a[:], b[:]))
    n_consts = inc(DV, c_dve)

    xt_evac = {}
    for tc in ([] if NOX else range(NCHUNK)):
        if tc >= 2:
            wt(GP, c_dve, xt_evac[tc - 2])
        GP.append(lambda e, tc=tc, b=xld[tc % 2]: e.dma_start(
            b[:], d_x[tc * 128:(tc + 1) * 128, :]))
        inc(GP, c_dx[tc % 2], 16)
        wt(PE, c_dx[tc % 2], (tc // 2 + 1) * 16)
        if tc == 0:
            wt(PE, c_dw)
        if tc >= 1:
            wt(PE, c_dve, xt_evac[tc - 1])
        for ic in range(8):
            PE.append(lambda e, b=xld[tc % 2], ic=ic: e.transpose(
                ps_tr[:, ic * 128:(ic + 1) * 128],
                b[:, ic * 128:(ic + 1) * 128], id_bf[:]))
        n_pe = inc(PE, c_pe)
        wt(DV, c_pe, n_pe)
        DV.append(lambda v, tc=tc: v.tensor_copy(
            bass.AP(xt, tc * 128, [[8 * S, 128], [S, 8], [1, 128]]),
            ps_tr[:].rearrange("p (c n) -> p c n", c=8)))
        xt_evac[tc] = inc(DV, c_dve)

    ph_0.close()

    if NOX:
        for tc in range(NCHUNK):
            xt_evac[tc] = 0

    kt = ph_a.enter_context(nc.sbuf_tensor("kt", [128, S], BF16))
    qt_s = ph_a.enter_context(nc.sbuf_tensor("qt_s", [128, T], BF16))
    vt_st = ph_a.enter_context(nc.sbuf_tensor("vt_st", [128, S], BF16))
    v_s = [ph_a.enter_context(nc.sbuf_tensor(f"v_s{i}", [128, NCHUNK * 130],
                                             BF16)) for i in range(2)]
    pt = [ph_a.enter_context(nc.sbuf_tensor(f"pt{i}", [128, 1024], BF16))
          for i in range(2)]
    lv_s = ph_a.enter_context(nc.sbuf_tensor("lv_s", [64, 2 * NQB * 512], F32))
    u1st = [ph_a.enter_context(nc.sbuf_tensor(f"u1st{i}", [64, 512], BF16))
            for i in range(2)]
    zln = ph_a.enter_context(nc.sbuf_tensor("zln", [128, 512], F32))
    zib = ph_a.enter_context(nc.sbuf_tensor("zib", [128, 1024], BF16))
    tmp_u = ph_a.enter_context(nc.sbuf_tensor("tmp_u", [128, 512], F32))
    wqkv = [ph_a.enter_context(nc.sbuf_tensor(f"wqkv{i}", [128, 3 * 1024],
                                              BF16)) for i in range(2)]
    dbg_snap = q_c[2]


    ph_aps = es.enter_context(contextlib.ExitStack())
    ps_s01 = ph_aps.enter_context(nc.psum_tensor("ps_s01", [128, 1024], F32))
    ps_s23 = ph_aps.enter_context(nc.psum_tensor("ps_s23", [128, 1024], F32))
    ps_u0 = ph_aps.enter_context(nc.psum_tensor("ps_u0", [128, 512], F32))
    ps_u1 = ph_aps.enter_context(nc.psum_tensor("ps_u1", [128, 512], F32))
    ps_g = ph_aps.enter_context(nc.psum_tensor("ps_g", [128, 512], F32))

    for i in range(2):
        DV.append(lambda v, vb=v_s[i]: v.memset(
            bass.AP(vb, 64, [[NCHUNK * 130, 128], [65, 64]]), 1.0))
    DV.append(lambda v: v.memset(zib[:], 0.0))
    n_vones = inc(DV, c_dve)

    wq_jobs = []
    for ch in range(FI // 128):
        wq_jobs.append((d_wg[ch * 128:(ch + 1) * 128, :],
                        d_sgq[ch * 128:(ch + 1) * 128, :], D))
    for ch in range(FI // 128):
        wq_jobs.append((d_wu[ch * 128:(ch + 1) * 128, :],
                        d_suq[ch * 128:(ch + 1) * 128, :], D))
    for ch in range(D // 128):
        for pc in range(4):
            wq_jobs.append((d_wd[ch * 128:(ch + 1) * 128,
                                 pc * 704:(pc + 1) * 704],
                            d_sdq[ch * 128:(ch + 1) * 128,
                                  pc * 704:(pc + 1) * 704], 704))
    wqs = {"load": 0, "quant": 0, "loadn": {}, "qn": {}, "qdn": {}}

    def emit_wq_loads(k):
        for _ in range(k):
            i = wqs["load"]
            if i >= len(wq_jobs):
                return
            if i - 2 >= 0:
                if i - 2 not in wqs["qdn"]:
                    return
                wt(SP, c_dve, wqs["qdn"][i - 2])
            src, _, fd = wq_jobs[i]
            SP.append(lambda e, b=wfq_in[i % 2], s=src, fd=fd: e.dma_start(
                b[:, 0:fd], s))
            wqs["loadn"][i] = inc(SP, c_dfw[i % 2], 16)
            wqs["load"] = i + 1

    def emit_wq_quant(k):
        if NOWQ:
            return
        for _ in range(k):
            i = wqs["quant"]
            if i >= wqs["load"]:
                return
            _, dst, fd = wq_jobs[i]
            wt(DV, c_dfw[i % 2], wqs["loadn"][i])
            if i >= 2:
                wt(DV, c_dsw[i % 2], 16 * ((i - 2) // 2 + 1))
            nq = emit_quant(wfq_in[i % 2][:, 0:fd], wfq_out[i % 2][:, 0:fd], fd)
            wqs["qdn"][i] = nq
            wt(SP, c_dve, nq)
            SP.append(lambda e, b=wfq_out[i % 2], d=dst, fd=fd: e.dma_start(
                d, b[:, 0:fd]))
            wqs["qn"][i] = inc(SP, c_dsw[i % 2], 16)
            wqs["quant"] = i + 1

    def emit_wqkv_load(g):
        if g >= 2:
            wt(GP, c_act, n_proj_done[g - 2])
        for j, nm in enumerate(["wqT", "wkT", "wvT"]):
            GP.append(lambda e, j=j, dt=d_w[nm], g=g: e.dma_start(
                wqkv[g % 2][:, j * 1024:(j + 1) * 1024].rearrange(
                    "p (c n) -> p c n", c=8),
                bass.AP(dt.tensor, g * 128, [[D, 128], [128 * D, 8], [1, 128]])))
            inc(GP, c_dwg[g % 2], 16)
        return c_dwg[g % 2].n

    def l_main_ap(qb, jl):
        if qb == 0 and jl == 0:
            return lbf["lm_first"][:]
        if qb == 3 and jl == 3:
            return lbf["lm_last"][:]
        off = 384 - jl * 128
        return lbf["lsh"][:, off:off + 512]

    n_wqkv = [0] * (NG + 1)
    n_proj_done = [0] * NG
    n_wqkv[0] = emit_wqkv_load(0)
    if not DELAY_WQ:
        emit_wq_loads(2)

    ng_run = NG
    if TRUNC == "p0":
        ng_run = 0
    elif isinstance(TRUNC, str) and TRUNC.startswith("g") and TRUNC[1:].isdigit():
        ng_run = int(TRUNC[1:])

    def emit_block():
        with nc.Block() as blk:
            @blk.sync
            def _(e):
                for f in SP:
                    f(e)

            @blk.gpsimd
            def _(e):
                for f in GP:
                    f(e)

            @blk.tensor
            def _(e):
                for f in PE:
                    f(e)

            @blk.scalar
            def _(e):
                for f in SA:
                    f(e)

            @blk.vector
            def _(e):
                for f in DV:
                    f(e)

    def trunc_finish():
        if TRUNC == "p0":
            while wqs["quant"] < len(wq_jobs):
                emit_wq_loads(2)
                emit_wq_quant(2)
        while wqs["quant"] < wqs["load"]:
            emit_wq_quant(4)
        if DEBUG_DUMP:
            d_dbg_wdq2 = nc.dram_tensor("dbg_wdq", [D, FI], BF16,
                                        kind="ExternalOutput").ap()
            wt(GP, c_dsw[0], 16 * ((len(wq_jobs) + 1) // 2))
            wt(GP, c_dsw[1], 16 * (len(wq_jobs) // 2))
            for ch in range(FI // 128):
                GP.append(lambda e, ch=ch: e.dma_start(
                    wfq_out[0][:], d_sgq[ch * 128:(ch + 1) * 128, :]))
                inc(GP, c_dbo, 16)
                wt(GP, c_dbo, c_dbo.n)
                GP.append(lambda e, ch=ch: e.dma_start(
                    d_dbg_wgq[ch * 128:(ch + 1) * 128, :], wfq_out[0][:]))
                inc(GP, c_dbo, 16)
                wt(GP, c_dbo, c_dbo.n)
            for ch in range(D // 128):
                for pc in range(4):
                    GP.append(lambda e, ch=ch, pc=pc: e.dma_start(
                        wfq_out[0][:, 0:704],
                        d_sdq[ch * 128:(ch + 1) * 128, pc * 704:(pc + 1) * 704]))
                    inc(GP, c_dbo, 16)
                    wt(GP, c_dbo, c_dbo.n)
                    GP.append(lambda e, ch=ch, pc=pc: e.dma_start(
                        d_dbg_wdq2[ch * 128:(ch + 1) * 128,
                                   pc * 704:(pc + 1) * 704],
                        wfq_out[0][:, 0:704]))
                    inc(GP, c_dbo, 16)
                    wt(GP, c_dbo, c_dbo.n)
            wt(SP, c_dbg, c_dbg.n)
        ys_ = wfq_in[0]
        DV.append(lambda v: v.memset(ys_[:], 0.0))
        nd = inc(DV, c_dve)
        wt(SP, c_dve, nd)
        for qq in range(NT):
            SP.append(lambda e, qq=qq: e.dma_start(
                d_out[qq * 128:(qq + 1) * 128, :], ys_[:]))
            inc(SP, c_dout, 16)
        wt(SP, c_dout, 16 * NT)
        emit_block()
        es.close()

    last_comb = n_consts
    last_exp = 0
    v_evac = [0, 0]
    av_done = [0, 0]
    last_av_slot = [0, 0]

    CUT = [False]

    for g in range(ng_run):
        wsel = wqkv[g % 2]
        vsel = v_s[g % 2]
        wt(PE, c_dwg[g % 2], n_wqkv[g])
        if g == 0:
            wt(PE, c_dve, xt_evac[NCHUNK - 1])
            wt(PE, c_dve, n_consts)
        else:
            wt(PE, c_exp, last_exp)
            wt(PE, c_dve, last_comb)
        proj_jobs = ([("k", fb, kt, S) for fb in range(8)]
                     + [("q", fb, qt_s, T) for fb in range(4)]
                     + [("v", fb, vt_st, S) for fb in range(8)])
        pj_evac = []
        for jn, (kind, fb, dst, span) in enumerate(proj_jobs):
            wj = {"q": 0, "k": 1, "v": 2}[kind]
            bank = ps_g[:, 0:512] if jn % 2 == 0 else ps_s01[:, 0:512]
            col = fb * 512
            if jn >= 2:
                wt(PE, c_act, pj_evac[jn - 2])
            for ic in range(8):
                PE.append(lambda e, w=wsel, wj=wj, ic=ic, b=bank, col=col:
                          e.matmul(
                    b, w[:, wj * 1024 + ic * 128:wj * 1024 + (ic + 1) * 128],
                    xt[:, ic * S + col:ic * S + col + 512],
                    start=(ic == 0), stop=(ic == 7)))
            n_mm = inc(PE, c_pe)
            if DEBUG_DUMP == 2 and g == 0 and kind == "q" and fb == 0:
                wt(DV, c_pe, n_mm)
                DV.append(lambda v, b=bank: v.tensor_copy(
                    dbg_snap[:, 512:1024], b))
                n_snapq = inc(DV, c_dve)
                wt(SA, c_dve, n_snapq)
            wt(SA, c_pe, n_mm)
            scale = 0.125 if kind == "q" else 1.0
            bcol = {"q": g, "k": 8 + g, "v": 16 + g}[kind]
            SA.append(lambda a, b=bank, d=dst, col=col, bcol=bcol, scale=scale:
                      a.activation(d[:, col:col + 512], b, AF.Identity,
                                   bias=b_qkv[:, bcol:bcol + 1], scale=scale))
            pj_evac.append(inc(SA, c_act))
        n_proj_all = pj_evac[-1]
        n_proj_done[g] = n_proj_all
        if TRUNC == f"G{g}p":
            trunc_finish()
            return nc
        if g == 0 and TRUNC == "gp":
            trunc_finish()
            return nc

        wt(PE, c_act, n_proj_all)
        for batch in range(4):
            if batch > 0:
                wt(PE, c_dve, v_evac[g % 2])
            elif g > 0:
                wt(PE, c_dve, v_evac[(g - 1) % 2])
            for j in range(8):
                tc = batch * 8 + j
                PE.append(lambda e, tc=tc, j=j: e.transpose(
                    ps_tr[:, j * 128:(j + 1) * 128],
                    vt_st[:, tc * 128:(tc + 1) * 128], id_bf[:]))
            n_tr = inc(PE, c_pe)
            wt(DV, c_pe, n_tr)
            if g == 0 and batch == 0:
                wt(DV, c_dve, n_vones)
            if g >= 2 and batch == 0:
                wt(DV, c_pe, av_done[g % 2])
            DV.append(lambda v, batch=batch, vb=vsel: v.tensor_copy(
                bass.AP(vb, batch * 8 * 130,
                        [[NCHUNK * 130, 128], [130, 8], [65, 2], [1, 64]]),
                ps_tr[:].rearrange("p (c h n) -> p c h n", c=8, h=2)))
            v_evac[g % 2] = inc(DV, c_dve)

        if g == 0 and TRUNC == "gv":
            trunc_finish()
            return nc
        if TRUNC == f"G{g}v":
            trunc_finish()
            return nc
        if g + 1 < NG:
            n_wqkv[g + 1] = emit_wqkv_load(g + 1)
        if not DELAY_WQ:
            emit_wq_loads(3)
        if TRUNC == f"G{g}w":
            trunc_finish()
            return nc

        if DEBUG_DUMP == 2 and g == 0:
            wt(DV, c_act, n_proj_all)
            DV.append(lambda v: v.tensor_copy(dbg_snap[:, 0:512],
                                              kt[:, 1024:1536]))
            DV.append(lambda v: v.tensor_copy(dbg_snap[:, 512:1024],
                                              qt_s[:, 0:512]))
            inc(DV, c_dve)

        wt(PE, c_dve, v_evac[g % 2])
        lv_evacs = []
        for h in range(2):
            for qb in range(NQB):
                bank = [ps_u0, ps_u1][(h * NQB + qb) % 2]
                if len(lv_evacs) >= 2:
                    wt(PE, c_dve, lv_evacs[-2])
                elif g > 0 and len(lv_evacs) == 0:
                    wt(PE, c_dve, last_comb)

                def vstat(vb, jc, h):
                    return vb[:, jc * 130 + h * 65:jc * 130 + h * 65 + 64]
                for jl in range(4):
                    jc = qb * 4 + jl
                    PE.append(lambda e, b=bank, vb=vsel, jc=jc, h=h, qb=qb,
                              jl=jl: e.matmul(
                        b[0:64, :], vstat(vb, jc, h),
                        l_main_ap(qb, jl), start=(jl == 0), stop=False))
                jc_pre = (qb * 4 - 1) % NCHUNK
                pre_t = lbf["lpre0"] if qb == 0 else lbf["lpre_sh"]
                PE.append(lambda e, b=bank, vb=vsel, jc=jc_pre, h=h, t=pre_t:
                          e.matmul(
                    b[0:64, 0:128], vstat(vb, jc, h),
                    t[:], start=False, stop=False))
                jc_post = (qb * 4 + 4) % NCHUNK
                post_t = lbf["lpost3"] if qb == 3 else lbf["lpost_sh"]
                PE.append(lambda e, b=bank, vb=vsel, jc=jc_post, h=h,
                          t=post_t: e.matmul(
                    b[0:64, 384:512], vstat(vb, jc, h),
                    t[:], start=False, stop=True))
                n_l = inc(PE, c_pe)
                wt(DV, c_pe, n_l)
                DV.append(lambda v, b=bank, h=h, qb=qb: v.tensor_copy(
                    lv_s[0:64, (h * NQB + qb) * 512:(h * NQB + qb + 1) * 512],
                    b[0:64, :]))
                lv_evacs.append(inc(DV, c_dve))
        n_lv_all = lv_evacs[-1]

        if not DELAY_WQ:
            emit_wq_quant(3)
        if g == 0 and TRUNC == "gl":
            trunc_finish()
            return nc
        if TRUNC == f"G{g}l":
            trunc_finish()
            return nc

        for qb in range(NQB):
            exp_n = {}
            avn = {}
            for kc in range(NKC):
                sb = [ps_s01, ps_s23][kc % 2]
                if kc >= 2:
                    wt(PE, c_exp, exp_n[kc - 2])
                elif qb == 0 and kc < 2:
                    wt(PE, c_dve, n_lv_all)
                if qb > 0 and kc < 2:
                    wt(PE, c_dve, last_comb)
                PE.append(lambda e, sb=sb, kc=kc, qb=qb: e.matmul(
                    sb[:, 0:512], kt[0:64, kc * 128:(kc + 1) * 128],
                    qt_s[0:64, qb * 512:(qb + 1) * 512],
                    start=True, stop=True, tile_position=(0, 0)))
                PE.append(lambda e, sb=sb, kc=kc, qb=qb: e.matmul(
                    sb[:, 512:1024], kt[64:128, kc * 128:(kc + 1) * 128],
                    qt_s[64:128, qb * 512:(qb + 1) * 512],
                    start=True, stop=True, tile_position=(64, 0)))
                n_s = inc(PE, c_pe)
                wt(SA, c_pe, n_s)
                wt(SA, c_pe, last_av_slot[kc % 2])
                SA.append(lambda a, sb=sb, slot=kc % 2: a.activation(
                    pt[slot][:], sb[:], AF.Exp))
                exp_n[kc] = inc(SA, c_exp)
                last_exp = exp_n[kc]
                wt(PE, c_exp, exp_n[kc])
                PE.append(lambda e, vb=vsel, kc=kc, slot=kc % 2: e.matmul(
                    ps_u0[0:65, :], vb[:, kc * 130:kc * 130 + 65],
                    pt[slot][:, 0:512], start=(kc == 0), stop=(kc == NKC - 1)))
                PE.append(lambda e, vb=vsel, kc=kc, slot=kc % 2: e.matmul(
                    ps_u1[0:65, :], vb[:, kc * 130 + 65:kc * 130 + 130],
                    pt[slot][:, 512:1024], start=(kc == 0),
                    stop=(kc == NKC - 1)))
                avn[kc] = inc(PE, c_pe)
                last_av_slot[kc % 2] = avn[kc]

                if g == 0 and TRUNC == f"gq{qb}k{kc}":
                    trunc_finish()
                    return nc
            av_done[g % 2] = avn[NKC - 1]
            wt(SA, c_pe, avn[NKC - 1])
            for h in range(2):
                ub = [ps_u0, ps_u1][h]
                SA.append(lambda a, ub=ub: a.activation(
                    zln[64:65, 0:512], ub[64:65, :], AF.Ln))
                inc(SA, c_act)
                SA.append(lambda a, h=h: a.activation(
                    zib[64:65, h * 512:(h + 1) * 512], zln[64:65, 0:512],
                    AF.Exp, bias=lc_ln06[64:65, 0:1], scale=-1.0))
                inc(SA, c_act)
            n_zi = c_act.n
            wt(PE, c_act, n_zi)
            wt(PE, c_exp, last_exp)
            for h in range(2):
                PE.append(lambda e, h=h: e.matmul(
                    ps_s01[0:64, h * 512:(h + 1) * 512], ones_all[64:65, 0:64],
                    zib[64:65, h * 512:(h + 1) * 512], start=True, stop=True))
            n_zb = inc(PE, c_pe)
            wt(DV, c_pe, n_zb)
            for h in range(2):
                ub = [ps_u0, ps_u1][h]
                zb_ap = ps_s01[0:64, h * 512:(h + 1) * 512]
                DV.append(lambda v, ub=ub: v.tensor_copy(
                    tmp_u[0:64, :], ub[0:64, :]))
                DV.append(lambda v, zb=zb_ap: v.tensor_tensor(
                    tmp_u[0:64, :], tmp_u[0:64, :], zb, ALU.mult))
                if h == 0:
                    DV.append(lambda v, qb=qb, g=g: v.tensor_tensor(
                        ut[0:64, g * T + qb * 512:g * T + (qb + 1) * 512],
                        tmp_u[0:64, :],
                        lv_s[0:64, qb * 512:(qb + 1) * 512], ALU.add))
                else:
                    sidx = (g * NQB + qb) % 2
                    if g * NQB + qb >= 2:
                        wt(DV, c_reloc[sidx], 16 * ((g * NQB + qb) // 2))
                    DV.append(lambda v, qb=qb, sidx=sidx: v.tensor_tensor(
                        u1st[sidx][:, :], tmp_u[0:64, :],
                        lv_s[0:64, (NQB + qb) * 512:(NQB + qb + 1) * 512],
                        ALU.add))
            last_comb = inc(DV, c_dve)
            if g == 0 and TRUNC == f"gq{qb}nr":
                trunc_finish()
                return nc
            sidx = (g * NQB + qb) % 2
            wt(SA, c_dve, last_comb)
            SA.append(lambda a, qb=qb, g=g, sidx=sidx: a.dma_start(
                ut[64:128, g * T + qb * 512:g * T + (qb + 1) * 512],
                u1st[sidx][:, :]))
            inc(SA, c_reloc[sidx], 16)
            if g == 0 and TRUNC == f"gq{qb}":
                trunc_finish()
                return nc
            if TRUNC == f"G{g}q{qb}":
                trunc_finish()
                return nc
            if not DELAY_WQ:
                emit_wq_quant(1)
                emit_wq_loads(2)
        if TRUNC == f"G{g}":
            trunc_finish()
            return nc

    if TRUNC is not None and (TRUNC == "p0" or TRUNC.startswith("g")):
        trunc_finish()
        return nc

    ph_a.close()
    ph_aps.close()

    ph_b = es.enter_context(contextlib.ExitStack())
    wo_s = ph_b.enter_context(nc.sbuf_tensor("wo_s", [128, 8 * D], BF16))
    acc = es.enter_context(nc.sbuf_tensor("acc", [128, NT * D], F32, side="right"))
    xres = [ph_b.enter_context(nc.sbuf_tensor(f"xres{i}", [128, D], F32))
            for i in range(2)]
    rs_s = ph_b.enter_context(nc.sbuf_tensor("rs_s", [128, D], F32))
    sq_s = q_ax
    mr_s = ph_b.enter_context(nc.sbuf_tensor("mr_s", [128, 1], F32))
    nm_s = ph_b.enter_context(nc.sbuf_tensor("nm_s", [128, 1], F32))
    ssq_s = ph_b.enter_context(nc.sbuf_tensor("ssq_s", [128, 1], F32))
    sd_s = ph_b.enter_context(nc.sbuf_tensor("sd_s", [128, 1], F32))
    rstd_s = ph_b.enter_context(nc.sbuf_tensor("rstd_s", [128, 1], F32))
    b2_s = ph_b.enter_context(nc.sbuf_tensor("b2_s", [128, 1], F32))
    xq_s = ph_b.enter_context(nc.sbuf_tensor("xq_s", [128, D], BF16))
    xqT = es.enter_context(nc.sbuf_tensor("xqT", [128, 8 * T], BF16, side="right"))

    ph_bps = es.enter_context(contextlib.ExitStack())
    ps_o = ph_bps.enter_context(nc.psum_tensor("ps_o", [128, 1024], F32))

    wt(GP, c_dve, last_comb)
    wt(GP, c_reloc[0], 16 * NG * NQB // 2)
    wt(GP, c_reloc[1], 16 * NG * NQB // 2)
    GP.append(lambda e: e.dma_start(
        wo_s[:].rearrange("p (c n) -> p c n", c=8),
        bass.AP(d_w["woT"].tensor, 0, [[D, 128], [128 * D, 8], [1, D]])))
    n_wo = inc(GP, c_dwo, 16)

    if DEBUG_DUMP == 2:
        SP.append(lambda e: e.dma_start(d_dbg_kt[:, 0:512],
                                        dbg_snap[:, 0:512]))
        inc(SP, c_dbg, 16)
        SP.append(lambda e: e.dma_start(d_dbg_qt[:, 0:512],
                                        dbg_snap[:, 512:1024]))
        inc(SP, c_dbg, 16)
    if DEBUG_DUMP:
        wt(SP, c_dve, last_comb)
        for par in range(2):
            wt(SP, c_reloc[par], 16 * NG * NQB // 2)
        SP.append(lambda e: e.dma_start(d_dbg_ut, ut[:]))
        inc(SP, c_dbg, 16)

    res_n = {}
    xqT_n = {}
    last_sq_act = 0
    last_x1_act = 0
    last_xtr = 0
    for q in range(NT):
        if q >= 2:
            wt(SA, c_dve, res_n[q - 2])
        SA.append(lambda e, q=q: e.dma_start(
            xres[q % 2][:], d_x[q * 128:(q + 1) * 128, :]))
        inc(SA, c_dxf[q % 2], 16)
        wt(PE, c_dwo, n_wo)
        if q == 0:
            wt(PE, c_dve, last_comb)
            wt(PE, c_reloc[0], 16 * NG * NQB // 2)
            wt(PE, c_reloc[1], 16 * NG * NQB // 2)
            wt(PE, c_dbo, 16)
        else:
            wt(PE, c_dve, res_n[q - 1])
        for nb in range(2):
            for ic in range(8):
                PE.append(lambda e, q=q, nb=nb, ic=ic: e.matmul(
                    ps_o[:, nb * 512:(nb + 1) * 512],
                    ut[:, ic * T + q * 128:ic * T + (q + 1) * 128],
                    wo_s[:, ic * D + nb * 512:ic * D + (nb + 1) * 512],
                    start=(ic == 0), stop=False))
            PE.append(lambda e, nb=nb: e.matmul(
                ps_o[:, nb * 512:(nb + 1) * 512], ones_all[0:1, 0:128],
                bo_row[:, nb * 512:(nb + 1) * 512], start=False, stop=True))
        n_op = inc(PE, c_pe)
        wt(DV, c_pe, n_op)
        wt(DV, c_dxf[q % 2], (q // 2 + 1) * 16)
        if q >= 1:
            wt(DV, c_act, last_sq_act)
        DV.append(lambda v, q=q: v.tensor_tensor(
            rs_s[:], ps_o[:], xres[q % 2][:], ALU.add))
        res_n[q] = inc(DV, c_dve)
        DV.append(lambda v: v.tensor_reduce(mr_s[:], rs_s[:], axis=AX.X,
                                            op=ALU.add))
        DV.append(lambda v: v.tensor_scalar(nm_s[:], mr_s[:], -1.0 / 1024.0,
                                            None, ALU.mult))
        n_nm = inc(DV, c_dve)
        wt(SA, c_dve, n_nm)
        SA.append(lambda a: a.activation(sq_s[:], rs_s[:], AF.Square,
                                         bias=nm_s[:, 0:1], scale=1.0,
                                         accum_out=ssq_s[:]))
        last_sq_act = inc(SA, c_act)
        SA.append(lambda a: a.activation(sd_s[:], ssq_s[:], AF.Ln,
                                         bias=lc_eps[:, 0:1], scale=1.0 / 1024.0))
        SA.append(lambda a: a.activation(rstd_s[:], sd_s[:], AF.Exp,
                                         bias=0.0, scale=-0.5))
        n_sd = inc(SA, c_act)
        wt(DV, c_act, n_sd)
        DV.append(lambda v: v.tensor_scalar(b2_s[:], mr_s[:], rstd_s[:, 0:1],
                                            -1.0 / 1024.0, ALU.mult, ALU.mult))
        n_rstd = inc(DV, c_dve)
        wt(SA, c_dve, n_rstd)
        SA.append(lambda a, q=q: a.activation(
            acc[:, q * D:(q + 1) * D], rs_s[:], AF.Identity,
            bias=b2_s[:, 0:1], scale=rstd_s[:, 0:1]))
        last_x1_act = inc(SA, c_act)
        wt(DV, c_act, last_x1_act)
        if q >= 1:
            wt(DV, c_pe, last_xtr)
        n_xq = emit_quant(acc[:, q * D:(q + 1) * D], xq_s[:], D)
        wt(PE, c_dve, n_xq)
        if q >= 1:
            wt(PE, c_dve, xqT_n[q - 1])
        for ic in range(8):
            PE.append(lambda e, ic=ic: e.transpose(
                ps_tr[:, ic * 128:(ic + 1) * 128],
                xq_s[:, ic * 128:(ic + 1) * 128], id_bf[:]))
        last_xtr = inc(PE, c_pe)
        wt(DV, c_pe, last_xtr)
        DV.append(lambda v, q=q: v.tensor_copy(
            bass.AP(xqT, q * 128, [[8 * T, 128], [T, 8], [1, 128]]),
            ps_tr[:].rearrange("p (c n) -> p c n", c=8)))
        xqT_n[q] = inc(DV, c_dve)
        emit_wq_loads(2)
        emit_wq_quant(1)

    if DEBUG_DUMP:
        wt(SP, c_dve, xqT_n[NT - 1])
        wt(SP, c_act, last_x1_act)
        SP.append(lambda e: e.dma_start(d_dbg_acc, acc[:]))
        inc(SP, c_dbg, 16)
        SP.append(lambda e: e.dma_start(d_dbg_xqt, xqT[:]))
        inc(SP, c_dbg, 16)

    while wqs["quant"] < len(wq_jobs):
        emit_wq_loads(2)
        emit_wq_quant(2)

    if TRUNC == "p2":
        trunc_finish()
        return nc

    ph_b.close()
    ph_bps.close()
    ut_scope.close()

    ph_f = es.enter_context(contextlib.ExitStack())
    wgT = [ph_f.enter_context(nc.sbuf_tensor(f"wgT{i}", [128, 8 * 512], BF16))
           for i in range(2)]
    wuT = [ph_f.enter_context(nc.sbuf_tensor(f"wuT{i}", [128, 8 * 512], BF16))
           for i in range(2)]
    wdT = [ph_f.enter_context(nc.sbuf_tensor(f"wdT{i}", [128, 4 * 1024], BF16))
           for i in range(2)]
    gs_s = ph_f.enter_context(nc.sbuf_tensor("gs_s", [128, 512], F32))
    h_s = ph_f.enter_context(nc.sbuf_tensor("h_s", [128, 512], F32))
    hq_s = ph_f.enter_context(nc.sbuf_tensor("hq_s", [128, 512], BF16))
    hqT_s = ph_f.enter_context(nc.sbuf_tensor("hqT_s", [128, 512], BF16))
    y_s = ph_f.enter_context(nc.sbuf_tensor("y_s", [128, D], F32))
    mr2 = ph_f.enter_context(nc.sbuf_tensor("mr2", [128, 1], F32))
    nm2 = ph_f.enter_context(nc.sbuf_tensor("nm2", [128, 1], F32))
    ssq2 = ph_f.enter_context(nc.sbuf_tensor("ssq2", [128, 1], F32))
    sd2 = ph_f.enter_context(nc.sbuf_tensor("sd2", [128, 1], F32))
    rstd2 = ph_f.enter_context(nc.sbuf_tensor("rstd2", [128, 1], F32))
    b22 = ph_f.enter_context(nc.sbuf_tensor("b22", [128, 1], F32))
    sq2 = q_ax

    ph_fps = es.enter_context(contextlib.ExitStack())
    ps_gate = ph_fps.enter_context(nc.psum_tensor("ps_gate", [128, 512], F32))
    ps_up = ph_fps.enter_context(nc.psum_tensor("ps_up", [128, 512], F32))
    ps_dn = ph_fps.enter_context(nc.psum_tensor("ps_dn", [128, 1024], F32))

    n_scr0 = 16 * ((len(wq_jobs) + 1) // 2)
    n_scr1 = 16 * (len(wq_jobs) // 2)

    def emit_slab_loads(s, buf, pe_free=0, act_free=0, dve_free=0):
        ns = FFN_SLABS[s]
        wt(SP, c_dsw[0], n_scr0)
        wt(SP, c_dsw[1], n_scr1)
        if pe_free:
            wt(SP, c_pe, pe_free)
        if act_free:
            wt(SP, c_act, act_free)
        if dve_free:
            wt(SP, c_dve, dve_free)
        for hc in range(8):
            SP.append(lambda e, s=s, hc=hc, ns=ns, b=wgT[buf]: e.dma_start(
                b[:, hc * 512:hc * 512 + ns],
                d_sgq[s * 512:s * 512 + ns, hc * 128:(hc + 1) * 128],
                transpose=True))
            inc(SP, c_dst[buf], 16)
        for hc in range(8):
            SP.append(lambda e, s=s, hc=hc, ns=ns, b=wuT[buf]: e.dma_start(
                b[:, hc * 512:hc * 512 + ns],
                d_suq[s * 512:s * 512 + ns, hc * 128:(hc + 1) * 128],
                transpose=True))
            inc(SP, c_dst[buf], 16)
        for icc in range(ns // 128):
            SP.append(lambda e, s=s, icc=icc, b=wdT[buf]: e.dma_start(
                b[:, icc * 1024:(icc + 1) * 1024],
                d_sdq[:, s * 512 + icc * 128:s * 512 + (icc + 1) * 128],
                transpose=True))
            inc(SP, c_dst[buf], 16)
        return c_dst[buf].n

    slab_ready = [0] * len(FFN_SLABS)
    slab_ready[0] = emit_slab_loads(0, 0, pe_free=c_pe.n, act_free=c_act.n,
                                    dve_free=c_dve.n)
    slab_last_pe = [0] * len(FFN_SLABS)
    if DEBUG_DUMP:
        for ch in range(FI // 128):
            SP.append(lambda e, ch=ch: e.dma_start(
                wfq_out[ch % 2][:], d_sgq[ch * 128:(ch + 1) * 128, :]))
            inc(SP, c_dbg, 16)
            wt(SP, c_dbg, c_dbg.n)
            SP.append(lambda e, ch=ch: e.dma_start(
                d_dbg_wgq[ch * 128:(ch + 1) * 128, :], wfq_out[ch % 2][:]))
            inc(SP, c_dbg, 16)
            wt(SP, c_dbg, c_dbg.n)
        wt(SP, c_dst[0], slab_ready[0])
        SP.append(lambda e: e.dma_start(d_dbg_wgt, wgT[0][:]))
        inc(SP, c_dbg, 16)
        SP.append(lambda e: e.dma_start(d_dbg_wdt, wdT[0][:]))
        inc(SP, c_dbg, 16)

    acc_fin = {}
    last_acc = 0
    last_silu = 0
    last_hmul = 0
    last_htr = 0
    last_dn = 0
    for s, ns in enumerate(FFN_SLABS):
        nic = ns // 128
        buf = s % 2
        if s + 1 < len(FFN_SLABS):
            slab_ready[s + 1] = emit_slab_loads(
                s + 1, (s + 1) % 2,
                pe_free=slab_last_pe[s - 1] if s >= 1 else 0)
        for tt in range(NT):
            wt(PE, c_dst[buf], slab_ready[s])
            if s == 0:
                wt(PE, c_dve, xqT_n[tt])
            if last_silu:
                wt(PE, c_act, last_silu)
            if last_hmul:
                wt(PE, c_dve, last_hmul)
            for wsel2, bank in ((wgT[buf], ps_gate), (wuT[buf], ps_up)):
                for ic in range(8):
                    PE.append(lambda e, w=wsel2, ic=ic, tt=tt, b=bank, ns=ns:
                              e.matmul(
                        b[:, 0:ns],
                        xqT[:, ic * T + tt * 128:ic * T + (tt + 1) * 128],
                        w[:, ic * 512:ic * 512 + ns],
                        start=(ic == 0), stop=(ic == 7)))
            n_gu = inc(PE, c_pe)
            wt(SA, c_pe, n_gu)
            if last_hmul:
                wt(SA, c_dve, last_hmul)
            SA.append(lambda a, ns=ns: a.activation(
                gs_s[:, 0:ns], ps_gate[:, 0:ns], AF.Silu))
            last_silu = inc(SA, c_act)
            wt(DV, c_act, last_silu)
            DV.append(lambda v, ns=ns: v.tensor_tensor(
                h_s[:, 0:ns], gs_s[:, 0:ns], ps_up[:, 0:ns], ALU.mult))
            last_hmul = inc(DV, c_dve)
            if last_htr:
                wt(DV, c_pe, last_htr)
            n_hq = emit_quant(h_s[:, 0:ns], hq_s[:, 0:ns], ns)
            wt(PE, c_dve, n_hq)
            for icc in range(nic):
                PE.append(lambda e, icc=icc: e.transpose(
                    ps_tr[:, icc * 128:(icc + 1) * 128],
                    hq_s[:, icc * 128:(icc + 1) * 128], id_bf[:]))
            last_htr = inc(PE, c_pe)
            wt(DV, c_pe, last_htr)
            if last_dn:
                wt(DV, c_pe, last_dn)
            DV.append(lambda v, nic=nic: v.tensor_copy(
                hqT_s[:, 0:nic * 128], ps_tr[:, 0:nic * 128]))
            n_hqT = inc(DV, c_dve)
            wt(PE, c_dve, n_hqT)
            if last_acc:
                wt(PE, c_dve, last_acc)
            for icc in range(nic):
                for nb in range(2):
                    PE.append(lambda e, icc=icc, nb=nb, b=wdT[buf], nic=nic:
                              e.matmul(
                        ps_dn[:, nb * 512:(nb + 1) * 512],
                        hqT_s[:, icc * 128:(icc + 1) * 128],
                        b[:, icc * 1024 + nb * 512:icc * 1024 + (nb + 1) * 512],
                        start=(icc == 0), stop=(icc == nic - 1)))
            last_dn = inc(PE, c_pe)
            slab_last_pe[s] = last_dn
            wt(DV, c_pe, last_dn)
            DV.append(lambda v, tt=tt: v.tensor_tensor(
                acc[:, tt * D:(tt + 1) * D], acc[:, tt * D:(tt + 1) * D],
                ps_dn[:], ALU.add))
            last_acc = inc(DV, c_dve)
            if s == len(FFN_SLABS) - 1:
                acc_fin[tt] = last_acc

    last_sq2 = 0
    for q in range(NT):
        wt(DV, c_dve, acc_fin[q])
        if q >= 1:
            wt(DV, c_act, last_sq2)
        src = acc[:, q * D:(q + 1) * D]
        DV.append(lambda v, s=src: v.tensor_reduce(mr2[:], s, axis=AX.X,
                                                   op=ALU.add))
        DV.append(lambda v: v.tensor_scalar(nm2[:], mr2[:], -1.0 / 1024.0,
                                            None, ALU.mult))
        n_nm = inc(DV, c_dve)
        wt(SA, c_dve, n_nm)
        SA.append(lambda a, s=src: a.activation(sq2[:], s, AF.Square,
                                                bias=nm2[:, 0:1], scale=1.0,
                                                accum_out=ssq2[:]))
        last_sq2 = inc(SA, c_act)
        SA.append(lambda a: a.activation(sd2[:], ssq2[:], AF.Ln,
                                         bias=lc_eps[:, 0:1], scale=1.0 / 1024.0))
        SA.append(lambda a: a.activation(rstd2[:], sd2[:], AF.Exp,
                                         bias=0.0, scale=-0.5))
        n_sd = inc(SA, c_act)
        wt(DV, c_act, n_sd)
        DV.append(lambda v: v.tensor_scalar(b22[:], mr2[:], rstd2[:, 0:1],
                                            -1.0 / 1024.0, ALU.mult, ALU.mult))
        n_rstd = inc(DV, c_dve)
        wt(SA, c_dve, n_rstd)
        if q >= 1:
            wt(SA, c_dout, 16 * q)
        SA.append(lambda a, s=src: a.activation(y_s[:], s, AF.Identity,
                                                bias=b22[:, 0:1],
                                                scale=rstd2[:, 0:1]))
        n_y = inc(SA, c_act)
        wt(SP, c_act, n_y)
        SP.append(lambda e, q=q: e.dma_start(
            d_out[q * 128:(q + 1) * 128, :], y_s[:]))
        inc(SP, c_dout, 16)

    wt(SP, c_dout, 16 * NT)

    with nc.Block() as blk:
        @blk.sync
        def _(e):
            for f in SP:
                f(e)

        @blk.gpsimd
        def _(e):
            for f in GP:
                f(e)

        @blk.tensor
        def _(e):
            for f in PE:
                f(e)

        @blk.scalar
        def _(e):
            for f in SA:
                f(e)

        @blk.vector
        def _(e):
            for f in DV:
                f(e)

    es.close()
    return nc


def kernel(L0_attn_x, L0_attn_Wq, L0_attn_bq, L0_attn_Wk, L0_attn_bk,
           L0_attn_Wv, L0_attn_bv, L0_attn_Wo, L0_attn_bo,
           L0_ffn_w_gate, L0_ffn_w_up, L0_ffn_w_down):
    x = np.ascontiguousarray(np.asarray(L0_attn_x, dtype=np.float32))
    wqT = np.ascontiguousarray(np.asarray(L0_attn_Wq, np.float32).T)
    wkT = np.ascontiguousarray(np.asarray(L0_attn_Wk, np.float32).T)
    wvT = np.ascontiguousarray(np.asarray(L0_attn_Wv, np.float32).T)
    woT = np.ascontiguousarray(np.asarray(L0_attn_Wo, np.float32).T)
    ident = np.eye(128, dtype=np.float32).astype(ml_dtypes.bfloat16)

    nc = build_program()

    lconsts = [build_lband_consts(0), build_lband_consts(1)]
    in_maps = []
    for c in range(8):
        b, half = c // 2, c % 2
        xb = x[b]
        if half == 0:
            x_rot = xb
        else:
            x_rot = np.ascontiguousarray(np.concatenate([xb[T:], xb[:T]], 0))
        m = {
            "xb": x_rot, "wqT": wqT, "wkT": wkT, "wvT": wvT, "woT": woT,
            "bq": np.ascontiguousarray(np.asarray(L0_attn_bq, np.float32)),
            "bk": np.ascontiguousarray(np.asarray(L0_attn_bk, np.float32)),
            "bv": np.ascontiguousarray(np.asarray(L0_attn_bv, np.float32)),
            "bo": np.ascontiguousarray(np.asarray(L0_attn_bo, np.float32)),
            "wg": np.ascontiguousarray(np.asarray(L0_ffn_w_gate, np.float32)),
            "wu": np.ascontiguousarray(np.asarray(L0_ffn_w_up, np.float32)),
            "wd": np.ascontiguousarray(np.asarray(L0_ffn_w_down, np.float32)),
            "ident": ident,
        }
        for k, v in lconsts[half].items():
            m["L_" + k] = v
        in_maps.append(m)

    res = run_bass_kernel_spmd(nc, in_maps, core_ids=list(range(8)))
    out = np.empty((B, S, D), np.float32)
    for c in range(8):
        b, half = c // 2, c % 2
        out[b, half * T:(half + 1) * T] = res.results[c]["out"]
    return out


# version 02
# speedup vs baseline: 1.3612x; 1.3612x over previous
import contextlib
import numpy as np
import ml_dtypes

import concourse.bass as bass
import concourse.mybir as mybir
from concourse.bass_utils import run_bass_kernel_spmd

F32 = mybir.dt.float32
BF16 = mybir.dt.bfloat16
AF = mybir.ActivationFunctionType
ALU = mybir.AluOpType
AX = mybir.AxisListType

D = 1024
S = 4096
B = 4
T = 2048
NG = 8
NQB = 4
NKC = 32
NT = 16
NCHUNK = 32
FI = 2816
QUANT_THR = [0.25, 0.75, 1.25, 1.75, 2.5, 3.5, 5.0]
QUANT_W2 = [1.0, 1.0, 1.0, 1.0, 2.0, 2.0, 4.0]
FFN_SLABS = [512, 512, 512, 512, 512, 256]
TRUNC = None
NOWQ = False
NOX = False
DELAY_WQ = False
DEBUG_DUMP = False
LN06 = float(np.float32(np.log(0.6)))
INV06 = float(np.float32(1.0))


class Ctr:
    def __init__(self, es, nc, name):
        self.h = es.enter_context(nc.semaphore(name))
        self.n = 0


class Tag:

    def __init__(self, lst):
        self.lst = lst
        self.idx = len(lst) - 1

    def then_inc(self, h, k):
        f = self.lst[self.idx]

        def g(e, f=f, h=h, k=k):
            r = f(e)
            r.then_inc(h, k)
            return r

        self.lst[self.idx] = g


def inc(stream, ctr, k=1):
    Tag(stream).then_inc(ctr.h, k)
    ctr.n += k
    return ctr.n


def wt(stream, ctr, val=None):
    v = ctr.n if val is None else val
    if v > 0:
        stream.append(lambda e, h=ctr.h, vv=v: e.wait_ge(h, vv))


def build_lband_consts(half):
    w = np.array([np.exp(-(d * d) / (2.0 * 0.4 ** 2)) for d in range(-3, 4)])

    def l04(qo, jo):
        if jo < 0 or jo >= S or abs(qo - jo) > 3:
            return 0.0
        z = sum(w[dd + 3] for dd in range(-3, 4) if 0 <= qo + dd < S)
        return 0.4 * w[(jo - qo) + 3] / (z + 1e-10)

    q0 = half * T

    def orig(r):
        return (q0 + r) % S

    def band(dlt):
        if abs(dlt) > 3:
            return 0.0
        return 0.4 * w[-dlt + 3] / (w.sum() + 1e-10)

    pp, uu = np.meshgrid(np.arange(128), np.arange(896), indexing="ij")
    lsh = np.vectorize(band)(uu - 384 - pp)

    def main_tile(qb, jl):
        t = np.zeros((128, 512))
        for p in range(128):
            jr = (qb * 4 + jl) * 128 + p
            lo = max(0, p + 128 * jl * 0 + (jl * 128 + 0) * 0)
            for c in range(max(0, jl * 128 + p - 3), min(512, jl * 128 + p + 4)):
                t[p, c] = l04(orig(qb * 512 + c), orig(jr))
        return t

    lm_first = main_tile(0, 0)
    lm_last = main_tile(3, 3)

    def fringe(qb, pre):
        t = np.zeros((128, 128))
        if pre:
            jc = (qb * 4 - 1) % NCHUNK
            for p in range(125, 128):
                for c in range(0, 3):
                    t[p, c] = l04(orig(qb * 512 + c), orig(jc * 128 + p))
        else:
            jc = (qb * 4 + 4) % NCHUNK
            for p in range(0, 3):
                for c in range(125, 128):
                    t[p, c] = l04(orig(qb * 512 + 384 + c), orig(jc * 128 + p))
        return t

    out = {
        "lsh": lsh,
        "lm_first": lm_first, "lm_last": lm_last,
        "lpre0": fringe(0, True), "lpost3": fringe(3, False),
        "lpre_sh": fringe(1, True), "lpost_sh": fringe(1, False),
    }
    return {k: np.ascontiguousarray(v, np.float32) for k, v in out.items()}


def build_program():
    nc = bass.Bass("TRN2", target_bir_lowering=False, debug=False,
                   detect_race_conditions=False)

    d_x = nc.dram_tensor("xb", [S, D], F32, kind="ExternalInput").ap()
    d_w = {}
    for nm in ["wqT", "wkT", "wvT", "woT"]:
        d_w[nm] = nc.dram_tensor(nm, [D, D], F32, kind="ExternalInput").ap()
    d_b = {}
    for nm in ["bq", "bk", "bv", "bo"]:
        d_b[nm] = nc.dram_tensor(nm, [D], F32, kind="ExternalInput").ap()
    d_wg = nc.dram_tensor("wg", [FI, D], F32, kind="ExternalInput").ap()
    d_wu = nc.dram_tensor("wu", [FI, D], F32, kind="ExternalInput").ap()
    d_wd = nc.dram_tensor("wd", [D, FI], F32, kind="ExternalInput").ap()
    d_ident = nc.dram_tensor("ident", [128, 128], BF16, kind="ExternalInput").ap()
    lshapes = [("lsh", 896), ("lm_first", 512), ("lm_last", 512),
               ("lpre0", 128), ("lpost3", 128), ("lpre_sh", 128),
               ("lpost_sh", 128)]
    d_l = {nm: nc.dram_tensor("L_" + nm, [128, w], F32,
                              kind="ExternalInput").ap()
           for nm, w in lshapes}
    d_out = nc.dram_tensor("out", [T, D], F32, kind="ExternalOutput").ap()
    if DEBUG_DUMP == 2:
        d_dbg_kt = nc.dram_tensor("dbg_kt", [128, 512], BF16,
                                  kind="ExternalOutput").ap()
        d_dbg_qt = nc.dram_tensor("dbg_qt", [128, 512], BF16,
                                  kind="ExternalOutput").ap()
    if DEBUG_DUMP:
        d_dbg_ut = nc.dram_tensor("dbg_ut", [128, 8 * T], BF16,
                                  kind="ExternalOutput").ap()
        d_dbg_acc = nc.dram_tensor("dbg_acc", [128, NT * D], F32,
                                   kind="ExternalOutput").ap()
        d_dbg_xqt = nc.dram_tensor("dbg_xqt", [128, 8 * T], BF16,
                                   kind="ExternalOutput").ap()
        d_dbg_wgq = nc.dram_tensor("dbg_wgq", [FI, D], BF16,
                                   kind="ExternalOutput").ap()
        d_dbg_wgt = nc.dram_tensor("dbg_wgt", [128, 8 * 512], BF16,
                                   kind="ExternalOutput").ap()
        d_dbg_wdt = nc.dram_tensor("dbg_wdt", [128, 4 * 1024], BF16,
                                   kind="ExternalOutput").ap()
    d_sgq = nc.dram_tensor("scr_wgq", [FI, D], BF16, kind="Internal").ap()
    d_suq = nc.dram_tensor("scr_wuq", [FI, D], BF16, kind="Internal").ap()
    d_sdq = nc.dram_tensor("scr_wdq", [D, FI], BF16, kind="Internal").ap()

    es = contextlib.ExitStack()

    PE, SA, DV, SP, GP = [], [], [], [], []

    c_dx = [Ctr(es, nc, "dx0"), Ctr(es, nc, "dx1")]
    c_dxf = [Ctr(es, nc, "dxf0"), Ctr(es, nc, "dxf1")]
    c_dw = Ctr(es, nc, "dw")
    c_dwg = [Ctr(es, nc, "dwg0"), Ctr(es, nc, "dwg1")]
    c_dbo = Ctr(es, nc, "dbo")
    c_dbg = Ctr(es, nc, "dbg")
    c_dwo = Ctr(es, nc, "dwo")
    c_dfw = [Ctr(es, nc, "dfw0"), Ctr(es, nc, "dfw1")]
    c_dsw = [Ctr(es, nc, "dsw0"), Ctr(es, nc, "dsw1")]
    c_dst = [Ctr(es, nc, "dst0"), Ctr(es, nc, "dst1")]
    c_dout = Ctr(es, nc, "dout")
    c_reloc = [Ctr(es, nc, "creloc0"), Ctr(es, nc, "creloc1")]
    c_pe = Ctr(es, nc, "cpe")
    c_act = Ctr(es, nc, "cact")
    c_dve = Ctr(es, nc, "cdve")
    c_exp = Ctr(es, nc, "cexp")

    id_bf = es.enter_context(nc.sbuf_tensor("id_bf", [128, 128], BF16))
    ones_all = es.enter_context(nc.sbuf_tensor("ones_all", [128, 128], BF16))
    b_qkv = es.enter_context(nc.sbuf_tensor("b_qkv", [128, 24], F32))
    bo_row = es.enter_context(nc.sbuf_tensor("bo_row", [1, D], BF16))
    r6 = es.enter_context(nc.sbuf_tensor("r6", [128, 1], F32))
    lc_ln06 = es.enter_context(nc.sbuf_tensor("lc_ln06", [128, 1], F32))
    lc_eps = es.enter_context(nc.sbuf_tensor("lc_eps", [128, 1], F32))
    lbf = {nm: es.enter_context(nc.sbuf_tensor("lbf_" + nm, [128, w], BF16))
           for nm, w in lshapes}
    q_ax = es.enter_context(nc.sbuf_tensor("q_ax", [128, 1024], F32))
    q_sg = es.enter_context(nc.sbuf_tensor("q_sg", [128, 1024], F32))
    q_c = [es.enter_context(nc.sbuf_tensor(f"q_c{i}", [128, 1024], BF16))
           for i in range(3)]
    q_bm = es.enter_context(nc.sbuf_tensor("q_bm", [128, 64], F32))
    q_s = es.enter_context(nc.sbuf_tensor("q_s", [128, 64], F32))
    q_rs = es.enter_context(nc.sbuf_tensor("q_rs", [128, 64], F32))
    q_sh = es.enter_context(nc.sbuf_tensor("q_sh", [128, 64], F32))

    ps_tr = es.enter_context(nc.psum_tensor("ps_tr", [128, 1024], BF16))
    ut_scope = es.enter_context(contextlib.ExitStack())
    ut = ut_scope.enter_context(nc.sbuf_tensor("ut", [128, 8 * T], BF16))
    wfq_in = [ut_scope.enter_context(nc.sbuf_tensor(f"wfqi{i}", [128, 1024],
                                                    F32)) for i in range(2)]
    wfq_out = [ut_scope.enter_context(nc.sbuf_tensor(f"wfqo{i}", [128, 1024],
                                                     BF16)) for i in range(2)]

    def bc16(t, nblk):
        return bass.AP(t, 0, [[t.shape[1], 128], [1, nblk], [0, 16]])

    def r3d(ap_):
        return ap_.rearrange("p (b s) -> p b s", s=16)

    def emit_quant(src_ap, dst_ap, fd):
        nb = fd // 16
        ax = q_ax[:, 0:fd]
        sg = q_sg[:, 0:fd]
        bm = q_bm[:, 0:nb]
        cb = [q_c[i][:, 0:fd] for i in range(3)]
        DV.append(lambda v, a=src_ap, o=bm: v.tensor_reduce(
            o, a.rearrange("p (b s) -> p b s", s=16), axis=AX.X, op=ALU.max,
            apply_absolute_value=True))
        DV.append(lambda v: v.tensor_scalar(
            q_s[:, 0:64], q_bm[:, 0:64], r6[:, 0:1], 1e-12, ALU.mult, ALU.max))
        DV.append(lambda v: v.reciprocal(q_rs[:, 0:64], q_s[:, 0:64]))
        DV.append(lambda v: v.tensor_scalar(
            q_sh[:, 0:64], q_s[:, 0:64], 0.5, None, ALU.mult))
        DV.append(lambda v, i=src_ap, o=ax: v.tensor_scalar(
            o, i, -1.0, None, ALU.mult))
        DV.append(lambda v, i=src_ap, o=ax: v.tensor_tensor(o, o, i, ALU.max))
        DV.append(lambda v, i=src_ap, o=sg: v.tensor_scalar(
            o, i, 0.0, -2.0, ALU.is_lt, ALU.mult))
        DV.append(lambda v, o=sg: v.tensor_scalar(o, o, 1.0, None, ALU.add))
        DV.append(lambda v, o=ax, n=nb: v.tensor_tensor(
            r3d(o), r3d(o), bc16(q_rs, n), ALU.mult))
        DV.append(lambda v, i=ax, o=cb[0]: v.tensor_scalar(
            o, i, QUANT_THR[0], QUANT_W2[0], ALU.is_gt, ALU.mult))
        DV.append(lambda v, i=ax, o=cb[1]: v.tensor_scalar(
            o, i, QUANT_THR[1], QUANT_W2[1], ALU.is_gt, ALU.mult))
        DV.append(lambda v, a=cb[0], b=cb[1]: v.tensor_tensor(a, a, b, ALU.add))
        for k in (2, 3, 4, 5, 6):
            DV.append(lambda v, i=ax, o=cb[1], k=k: v.tensor_scalar(
                o, i, QUANT_THR[k], QUANT_W2[k], ALU.is_gt, ALU.mult))
            DV.append(lambda v, a=cb[0], b=cb[1]: v.tensor_tensor(
                a, a, b, ALU.add))
        DV.append(lambda v, g=cb[0], o=ax, n=nb: v.tensor_tensor(
            r3d(o), r3d(g), bc16(q_sh, n), ALU.mult))
        DV.append(lambda v, a=ax, b=sg, o=dst_ap: v.tensor_tensor(
            o, a, b, ALU.mult))
        return inc(DV, c_dve)

    ph_a = es.enter_context(contextlib.ExitStack())
    xt = ph_a.enter_context(nc.sbuf_tensor("xt", [128, 8 * S], BF16))
    ph_0 = es.enter_context(contextlib.ExitStack())
    xld = [ph_0.enter_context(nc.sbuf_tensor(f"xld{i}", [128, D], BF16))
           for i in range(2)]
    l32 = {nm: ph_0.enter_context(nc.sbuf_tensor("l32_" + nm, [128, w], F32))
           for nm, w in lshapes}

    SP.append(lambda e: e.dma_start(id_bf[:], d_ident))
    inc(SP, c_dw, 16)
    for nm, _w in lshapes:
        SP.append(lambda e, a=l32[nm], b=d_l[nm]: e.dma_start(a[:], b))
        inc(SP, c_dw, 16)
    for j, nm in enumerate(["bq", "bk", "bv"]):
        SP.append(lambda e, j=j, dt=d_b[nm]: e.dma_start(
            b_qkv[:, j * 8:(j + 1) * 8],
            bass.AP(dt.tensor, 0, [[1, 128], [128, 8]]),
            allow_slow_non_contiguous=True))
        inc(SP, c_dw, 16)
    GP.append(lambda e: e.dma_start(
        bo_row[:], bass.AP(d_b["bo"].tensor, 0, [[D, 1], [1, D]])))
    inc(GP, c_dbo, 16)

    wt(DV, c_dw)
    DV.append(lambda v: v.memset(r6[:], 6.0))
    DV.append(lambda v: v.memset(q_s[:], 1.0))
    DV.append(lambda v: v.memset(q_bm[:], 1.0))
    DV.append(lambda v: v.reciprocal(r6[:], r6[:]))
    DV.append(lambda v: v.memset(ones_all[:], 1.0))
    DV.append(lambda v: v.memset(lc_ln06[:], LN06))
    DV.append(lambda v: v.memset(lc_eps[:], 1e-5))
    DV.append(lambda v: v.tensor_scalar(b_qkv[:, 0:8], b_qkv[:, 0:8], 0.125,
                                        None, ALU.mult))
    for nm, _w in lshapes:
        DV.append(lambda v, a=lbf[nm], b=l32[nm]: v.tensor_copy(a[:], b[:]))
    n_consts = inc(DV, c_dve)

    xt_evac = {}
    for tc in ([] if NOX else range(NCHUNK)):
        if tc >= 2:
            wt(GP, c_dve, xt_evac[tc - 2])
        GP.append(lambda e, tc=tc, b=xld[tc % 2]: e.dma_start(
            b[:], d_x[tc * 128:(tc + 1) * 128, :]))
        inc(GP, c_dx[tc % 2], 16)
        wt(PE, c_dx[tc % 2], (tc // 2 + 1) * 16)
        if tc == 0:
            wt(PE, c_dw)
        if tc >= 1:
            wt(PE, c_dve, xt_evac[tc - 1])
        for ic in range(8):
            PE.append(lambda e, b=xld[tc % 2], ic=ic: e.transpose(
                ps_tr[:, ic * 128:(ic + 1) * 128],
                b[:, ic * 128:(ic + 1) * 128], id_bf[:]))
        n_pe = inc(PE, c_pe)
        wt(DV, c_pe, n_pe)
        DV.append(lambda v, tc=tc: v.tensor_copy(
            bass.AP(xt, tc * 128, [[8 * S, 128], [S, 8], [1, 128]]),
            ps_tr[:].rearrange("p (c n) -> p c n", c=8)))
        xt_evac[tc] = inc(DV, c_dve)

    ph_0.close()

    if NOX:
        for tc in range(NCHUNK):
            xt_evac[tc] = 0

    kt = ph_a.enter_context(nc.sbuf_tensor("kt", [128, S], BF16))
    qt_s = ph_a.enter_context(nc.sbuf_tensor("qt_s", [128, T], BF16))
    vt_st = ph_a.enter_context(nc.sbuf_tensor("vt_st", [128, S], BF16))
    v_s = [ph_a.enter_context(nc.sbuf_tensor(f"v_s{i}", [128, NCHUNK * 130],
                                             BF16)) for i in range(2)]
    pt = [ph_a.enter_context(nc.sbuf_tensor(f"pt{i}", [128, 1024], BF16))
          for i in range(2)]
    lv_s = ph_a.enter_context(nc.sbuf_tensor("lv_s", [64, 2 * NQB * 512], F32))
    u1st = [ph_a.enter_context(nc.sbuf_tensor(f"u1st{i}", [64, 512], BF16))
            for i in range(2)]
    zln = ph_a.enter_context(nc.sbuf_tensor("zln", [128, 512], F32))
    zib = ph_a.enter_context(nc.sbuf_tensor("zib", [128, 1024], BF16))
    tmp_u = ph_a.enter_context(nc.sbuf_tensor("tmp_u", [128, 512], F32))
    wqkv = [ph_a.enter_context(nc.sbuf_tensor(f"wqkv{i}", [128, 3 * 1024],
                                              BF16)) for i in range(2)]
    dbg_snap = q_c[2]


    ph_aps = es.enter_context(contextlib.ExitStack())
    ps_s01 = ph_aps.enter_context(nc.psum_tensor("ps_s01", [128, 1024], F32))
    ps_s23 = ph_aps.enter_context(nc.psum_tensor("ps_s23", [128, 1024], F32))
    ps_u0 = ph_aps.enter_context(nc.psum_tensor("ps_u0", [128, 512], F32))
    ps_u1 = ph_aps.enter_context(nc.psum_tensor("ps_u1", [128, 512], F32))
    ps_g = ph_aps.enter_context(nc.psum_tensor("ps_g", [128, 512], F32))

    for i in range(2):
        DV.append(lambda v, vb=v_s[i]: v.memset(
            bass.AP(vb, 64, [[NCHUNK * 130, 128], [65, 64]]), 1.0))
    DV.append(lambda v: v.memset(zib[:], 0.0))
    n_vones = inc(DV, c_dve)

    wq_jobs = []
    for ch in range(FI // 128):
        wq_jobs.append((d_wg[ch * 128:(ch + 1) * 128, :],
                        d_sgq[ch * 128:(ch + 1) * 128, :], D))
    for ch in range(FI // 128):
        wq_jobs.append((d_wu[ch * 128:(ch + 1) * 128, :],
                        d_suq[ch * 128:(ch + 1) * 128, :], D))
    for ch in range(D // 128):
        for pc in range(4):
            wq_jobs.append((d_wd[ch * 128:(ch + 1) * 128,
                                 pc * 704:(pc + 1) * 704],
                            d_sdq[ch * 128:(ch + 1) * 128,
                                  pc * 704:(pc + 1) * 704], 704))
    wqs = {"load": 0, "quant": 0, "loadn": {}, "qn": {}, "qdn": {}}

    def emit_wq_loads(k):
        for _ in range(k):
            i = wqs["load"]
            if i >= len(wq_jobs):
                return
            if i - 2 >= 0:
                if i - 2 not in wqs["qdn"]:
                    return
                wt(SP, c_dve, wqs["qdn"][i - 2])
            src, _, fd = wq_jobs[i]
            SP.append(lambda e, b=wfq_in[i % 2], s=src, fd=fd: e.dma_start(
                b[:, 0:fd], s))
            wqs["loadn"][i] = inc(SP, c_dfw[i % 2], 16)
            wqs["load"] = i + 1

    def emit_wq_quant(k):
        if NOWQ:
            return
        for _ in range(k):
            i = wqs["quant"]
            if i >= wqs["load"]:
                return
            _, dst, fd = wq_jobs[i]
            wt(DV, c_dfw[i % 2], wqs["loadn"][i])
            if i >= 2:
                wt(DV, c_dsw[i % 2], 16 * ((i - 2) // 2 + 1))
            nq = emit_quant(wfq_in[i % 2][:, 0:fd], wfq_out[i % 2][:, 0:fd], fd)
            wqs["qdn"][i] = nq
            wt(SP, c_dve, nq)
            SP.append(lambda e, b=wfq_out[i % 2], d=dst, fd=fd: e.dma_start(
                d, b[:, 0:fd]))
            wqs["qn"][i] = inc(SP, c_dsw[i % 2], 16)
            wqs["quant"] = i + 1

    def emit_wqkv_load(g):
        if g >= 2:
            wt(GP, c_act, n_proj_done[g - 2])
        for j, nm in enumerate(["wqT", "wkT", "wvT"]):
            GP.append(lambda e, j=j, dt=d_w[nm], g=g: e.dma_start(
                wqkv[g % 2][:, j * 1024:(j + 1) * 1024].rearrange(
                    "p (c n) -> p c n", c=8),
                bass.AP(dt.tensor, g * 128, [[D, 128], [128 * D, 8], [1, 128]])))
            inc(GP, c_dwg[g % 2], 16)
        return c_dwg[g % 2].n

    def l_main_ap(qb, jl):
        if qb == 0 and jl == 0:
            return lbf["lm_first"][:]
        if qb == 3 and jl == 3:
            return lbf["lm_last"][:]
        off = 384 - jl * 128
        return lbf["lsh"][:, off:off + 512]

    n_wqkv = [0] * (NG + 1)
    n_proj_done = [0] * NG
    n_wqkv[0] = emit_wqkv_load(0)
    if not DELAY_WQ:
        emit_wq_loads(2)

    ng_run = NG
    if TRUNC == "p0":
        ng_run = 0
    elif isinstance(TRUNC, str) and TRUNC.startswith("g") and TRUNC[1:].isdigit():
        ng_run = int(TRUNC[1:])

    def emit_block():
        with nc.Block() as blk:
            @blk.sync
            def _(e):
                for f in SP:
                    f(e)

            @blk.gpsimd
            def _(e):
                for f in GP:
                    f(e)

            @blk.tensor
            def _(e):
                for f in PE:
                    f(e)

            @blk.scalar
            def _(e):
                for f in SA:
                    f(e)

            @blk.vector
            def _(e):
                for f in DV:
                    f(e)

    def trunc_finish():
        if TRUNC == "p0":
            while wqs["quant"] < len(wq_jobs):
                emit_wq_loads(2)
                emit_wq_quant(2)
        while wqs["quant"] < wqs["load"]:
            emit_wq_quant(4)
        if DEBUG_DUMP:
            d_dbg_wdq2 = nc.dram_tensor("dbg_wdq", [D, FI], BF16,
                                        kind="ExternalOutput").ap()
            wt(GP, c_dsw[0], 16 * ((len(wq_jobs) + 1) // 2))
            wt(GP, c_dsw[1], 16 * (len(wq_jobs) // 2))
            for ch in range(FI // 128):
                GP.append(lambda e, ch=ch: e.dma_start(
                    wfq_out[0][:], d_sgq[ch * 128:(ch + 1) * 128, :]))
                inc(GP, c_dbo, 16)
                wt(GP, c_dbo, c_dbo.n)
                GP.append(lambda e, ch=ch: e.dma_start(
                    d_dbg_wgq[ch * 128:(ch + 1) * 128, :], wfq_out[0][:]))
                inc(GP, c_dbo, 16)
                wt(GP, c_dbo, c_dbo.n)
            for ch in range(D // 128):
                for pc in range(4):
                    GP.append(lambda e, ch=ch, pc=pc: e.dma_start(
                        wfq_out[0][:, 0:704],
                        d_sdq[ch * 128:(ch + 1) * 128, pc * 704:(pc + 1) * 704]))
                    inc(GP, c_dbo, 16)
                    wt(GP, c_dbo, c_dbo.n)
                    GP.append(lambda e, ch=ch, pc=pc: e.dma_start(
                        d_dbg_wdq2[ch * 128:(ch + 1) * 128,
                                   pc * 704:(pc + 1) * 704],
                        wfq_out[0][:, 0:704]))
                    inc(GP, c_dbo, 16)
                    wt(GP, c_dbo, c_dbo.n)
            wt(SP, c_dbg, c_dbg.n)
        ys_ = wfq_in[0]
        DV.append(lambda v: v.memset(ys_[:], 0.0))
        nd = inc(DV, c_dve)
        wt(SP, c_dve, nd)
        for qq in range(NT):
            SP.append(lambda e, qq=qq: e.dma_start(
                d_out[qq * 128:(qq + 1) * 128, :], ys_[:]))
            inc(SP, c_dout, 16)
        wt(SP, c_dout, 16 * NT)
        emit_block()
        es.close()

    last_comb = n_consts
    last_exp = 0
    v_evac = [0, 0]
    av_done = [0, 0]
    last_av_slot = [0, 0]

    CUT = [False]

    for g in range(ng_run):
        wsel = wqkv[g % 2]
        vsel = v_s[g % 2]
        wt(PE, c_dwg[g % 2], n_wqkv[g])
        if g == 0:
            wt(PE, c_dve, xt_evac[NCHUNK - 1])
            wt(PE, c_dve, n_consts)
        else:
            wt(PE, c_exp, last_exp)
            wt(PE, c_dve, last_comb)
        proj_jobs = ([("k", fb, kt, S) for fb in range(8)]
                     + [("q", fb, qt_s, T) for fb in range(4)]
                     + [("v", fb, vt_st, S) for fb in range(8)])
        pj_evac = []
        for jn, (kind, fb, dst, span) in enumerate(proj_jobs):
            wj = {"q": 0, "k": 1, "v": 2}[kind]
            bank = ps_g[:, 0:512] if jn % 2 == 0 else ps_s01[:, 0:512]
            col = fb * 512
            if jn >= 2:
                wt(PE, c_act, pj_evac[jn - 2])
            for ic in range(8):
                PE.append(lambda e, w=wsel, wj=wj, ic=ic, b=bank, col=col:
                          e.matmul(
                    b, w[:, wj * 1024 + ic * 128:wj * 1024 + (ic + 1) * 128],
                    xt[:, ic * S + col:ic * S + col + 512],
                    start=(ic == 0), stop=(ic == 7)))
            n_mm = inc(PE, c_pe)
            if DEBUG_DUMP == 2 and g == 0 and kind == "q" and fb == 0:
                wt(DV, c_pe, n_mm)
                DV.append(lambda v, b=bank: v.tensor_copy(
                    dbg_snap[:, 512:1024], b))
                n_snapq = inc(DV, c_dve)
                wt(SA, c_dve, n_snapq)
            wt(SA, c_pe, n_mm)
            scale = 0.125 if kind == "q" else 1.0
            bcol = {"q": g, "k": 8 + g, "v": 16 + g}[kind]
            SA.append(lambda a, b=bank, d=dst, col=col, bcol=bcol, scale=scale:
                      a.activation(d[:, col:col + 512], b, AF.Identity,
                                   bias=b_qkv[:, bcol:bcol + 1], scale=scale))
            pj_evac.append(inc(SA, c_act))
        n_proj_all = pj_evac[-1]
        n_proj_done[g] = n_proj_all
        if TRUNC == f"G{g}p":
            trunc_finish()
            return nc
        if g == 0 and TRUNC == "gp":
            trunc_finish()
            return nc

        wt(PE, c_act, n_proj_all)
        for batch in range(4):
            if batch > 0:
                wt(PE, c_dve, v_evac[g % 2])
            elif g > 0:
                wt(PE, c_dve, v_evac[(g - 1) % 2])
            for j in range(8):
                tc = batch * 8 + j
                PE.append(lambda e, tc=tc, j=j: e.transpose(
                    ps_tr[:, j * 128:(j + 1) * 128],
                    vt_st[:, tc * 128:(tc + 1) * 128], id_bf[:]))
            n_tr = inc(PE, c_pe)
            wt(DV, c_pe, n_tr)
            if g == 0 and batch == 0:
                wt(DV, c_dve, n_vones)
            if g >= 2 and batch == 0:
                wt(DV, c_pe, av_done[g % 2])
            DV.append(lambda v, batch=batch, vb=vsel: v.tensor_copy(
                bass.AP(vb, batch * 8 * 130,
                        [[NCHUNK * 130, 128], [130, 8], [65, 2], [1, 64]]),
                ps_tr[:].rearrange("p (c h n) -> p c h n", c=8, h=2)))
            v_evac[g % 2] = inc(DV, c_dve)

        if g == 0 and TRUNC == "gv":
            trunc_finish()
            return nc
        if TRUNC == f"G{g}v":
            trunc_finish()
            return nc
        if g + 1 < NG:
            n_wqkv[g + 1] = emit_wqkv_load(g + 1)
        if not DELAY_WQ:
            emit_wq_loads(3)
        if TRUNC == f"G{g}w":
            trunc_finish()
            return nc

        if DEBUG_DUMP == 2 and g == 0:
            wt(DV, c_act, n_proj_all)
            DV.append(lambda v: v.tensor_copy(dbg_snap[:, 0:512],
                                              kt[:, 1024:1536]))
            DV.append(lambda v: v.tensor_copy(dbg_snap[:, 512:1024],
                                              qt_s[:, 0:512]))
            inc(DV, c_dve)

        wt(PE, c_dve, v_evac[g % 2])
        lv_evacs = []
        for h in range(2):
            for qb in range(NQB):
                bank = [ps_u0, ps_u1][(h * NQB + qb) % 2]
                if len(lv_evacs) >= 2:
                    wt(PE, c_dve, lv_evacs[-2])
                elif g > 0 and len(lv_evacs) == 0:
                    wt(PE, c_dve, last_comb)

                def vstat(vb, jc, h):
                    return vb[:, jc * 130 + h * 65:jc * 130 + h * 65 + 64]
                for jl in range(4):
                    jc = qb * 4 + jl
                    PE.append(lambda e, b=bank, vb=vsel, jc=jc, h=h, qb=qb,
                              jl=jl: e.matmul(
                        b[0:64, :], vstat(vb, jc, h),
                        l_main_ap(qb, jl), start=(jl == 0), stop=False))
                jc_pre = (qb * 4 - 1) % NCHUNK
                pre_t = lbf["lpre0"] if qb == 0 else lbf["lpre_sh"]
                PE.append(lambda e, b=bank, vb=vsel, jc=jc_pre, h=h, t=pre_t:
                          e.matmul(
                    b[0:64, 0:128], vstat(vb, jc, h),
                    t[:], start=False, stop=False))
                jc_post = (qb * 4 + 4) % NCHUNK
                post_t = lbf["lpost3"] if qb == 3 else lbf["lpost_sh"]
                PE.append(lambda e, b=bank, vb=vsel, jc=jc_post, h=h,
                          t=post_t: e.matmul(
                    b[0:64, 384:512], vstat(vb, jc, h),
                    t[:], start=False, stop=True))
                n_l = inc(PE, c_pe)
                wt(DV, c_pe, n_l)
                DV.append(lambda v, b=bank, h=h, qb=qb: v.tensor_copy(
                    lv_s[0:64, (h * NQB + qb) * 512:(h * NQB + qb + 1) * 512],
                    b[0:64, :]))
                lv_evacs.append(inc(DV, c_dve))
        n_lv_all = lv_evacs[-1]

        if not DELAY_WQ:
            emit_wq_quant(3)
        if g == 0 and TRUNC == "gl":
            trunc_finish()
            return nc
        if TRUNC == f"G{g}l":
            trunc_finish()
            return nc

        for qb in range(NQB):
            exp_n = {}
            avn = {}
            def emit_scores(kc):
                nonlocal last_exp
                sb = [ps_s01, ps_s23][kc % 2]
                if kc >= 2:
                    wt(PE, c_exp, exp_n[kc - 2])
                elif qb == 0 and kc < 2:
                    wt(PE, c_dve, n_lv_all)
                if qb > 0 and kc < 2:
                    wt(PE, c_dve, last_comb)
                PE.append(lambda e, sb=sb, kc=kc, qb=qb: e.matmul(
                    sb[:, 0:512], kt[0:64, kc * 128:(kc + 1) * 128],
                    qt_s[0:64, qb * 512:(qb + 1) * 512],
                    start=True, stop=True, tile_position=(0, 0)))
                PE.append(lambda e, sb=sb, kc=kc, qb=qb: e.matmul(
                    sb[:, 512:1024], kt[64:128, kc * 128:(kc + 1) * 128],
                    qt_s[64:128, qb * 512:(qb + 1) * 512],
                    start=True, stop=True, tile_position=(64, 0)))
                n_s = inc(PE, c_pe)
                wt(SA, c_pe, n_s)
                wt(SA, c_pe, last_av_slot[kc % 2])
                SA.append(lambda a, sb=sb, slot=kc % 2: a.activation(
                    pt[slot][:], sb[:], AF.Exp))
                exp_n[kc] = inc(SA, c_exp)
                last_exp = exp_n[kc]

            def emit_av(kc):
                wt(PE, c_exp, exp_n[kc])
                PE.append(lambda e, vb=vsel, kc=kc, slot=kc % 2: e.matmul(
                    ps_u0[0:65, :], vb[:, kc * 130:kc * 130 + 65],
                    pt[slot][:, 0:512], start=(kc == 0), stop=(kc == NKC - 1)))
                PE.append(lambda e, vb=vsel, kc=kc, slot=kc % 2: e.matmul(
                    ps_u1[0:65, :], vb[:, kc * 130 + 65:kc * 130 + 130],
                    pt[slot][:, 512:1024], start=(kc == 0),
                    stop=(kc == NKC - 1)))
                avn[kc] = inc(PE, c_pe)
                last_av_slot[kc % 2] = avn[kc]

            emit_scores(0)
            for kc in range(1, NKC):
                emit_scores(kc)
                emit_av(kc - 1)
                if g == 0 and TRUNC == f"gq{qb}k{kc - 1}":
                    trunc_finish()
                    return nc
            emit_av(NKC - 1)
            av_done[g % 2] = avn[NKC - 1]
            wt(SA, c_pe, avn[NKC - 1])
            for h in range(2):
                ub = [ps_u0, ps_u1][h]
                SA.append(lambda a, ub=ub: a.activation(
                    zln[64:65, 0:512], ub[64:65, :], AF.Ln))
                inc(SA, c_act)
                SA.append(lambda a, h=h: a.activation(
                    zib[64:65, h * 512:(h + 1) * 512], zln[64:65, 0:512],
                    AF.Exp, bias=lc_ln06[64:65, 0:1], scale=-1.0))
                inc(SA, c_act)
            n_zi = c_act.n
            wt(PE, c_act, n_zi)
            wt(PE, c_exp, last_exp)
            for h in range(2):
                PE.append(lambda e, h=h: e.matmul(
                    ps_s01[0:64, h * 512:(h + 1) * 512], ones_all[64:65, 0:64],
                    zib[64:65, h * 512:(h + 1) * 512], start=True, stop=True))
            n_zb = inc(PE, c_pe)
            wt(DV, c_pe, n_zb)
            for h in range(2):
                ub = [ps_u0, ps_u1][h]
                zb_ap = ps_s01[0:64, h * 512:(h + 1) * 512]
                DV.append(lambda v, ub=ub: v.tensor_copy(
                    tmp_u[0:64, :], ub[0:64, :]))
                DV.append(lambda v, zb=zb_ap: v.tensor_tensor(
                    tmp_u[0:64, :], tmp_u[0:64, :], zb, ALU.mult))
                if h == 0:
                    DV.append(lambda v, qb=qb, g=g: v.tensor_tensor(
                        ut[0:64, g * T + qb * 512:g * T + (qb + 1) * 512],
                        tmp_u[0:64, :],
                        lv_s[0:64, qb * 512:(qb + 1) * 512], ALU.add))
                else:
                    sidx = (g * NQB + qb) % 2
                    if g * NQB + qb >= 2:
                        wt(DV, c_reloc[sidx], 16 * ((g * NQB + qb) // 2))
                    DV.append(lambda v, qb=qb, sidx=sidx: v.tensor_tensor(
                        u1st[sidx][:, :], tmp_u[0:64, :],
                        lv_s[0:64, (NQB + qb) * 512:(NQB + qb + 1) * 512],
                        ALU.add))
            last_comb = inc(DV, c_dve)
            if g == 0 and TRUNC == f"gq{qb}nr":
                trunc_finish()
                return nc
            sidx = (g * NQB + qb) % 2
            wt(SA, c_dve, last_comb)
            SA.append(lambda a, qb=qb, g=g, sidx=sidx: a.dma_start(
                ut[64:128, g * T + qb * 512:g * T + (qb + 1) * 512],
                u1st[sidx][:, :]))
            inc(SA, c_reloc[sidx], 16)
            if g == 0 and TRUNC == f"gq{qb}":
                trunc_finish()
                return nc
            if TRUNC == f"G{g}q{qb}":
                trunc_finish()
                return nc
            if not DELAY_WQ:
                emit_wq_quant(1)
                emit_wq_loads(2)
        if TRUNC == f"G{g}":
            trunc_finish()
            return nc

    if TRUNC is not None and (TRUNC == "p0" or TRUNC.startswith("g")):
        trunc_finish()
        return nc

    ph_a.close()
    ph_aps.close()

    ph_b = es.enter_context(contextlib.ExitStack())
    wo_s = ph_b.enter_context(nc.sbuf_tensor("wo_s", [128, 8 * D], BF16))
    acc = es.enter_context(nc.sbuf_tensor("acc", [128, NT * D], F32, side="right"))
    xres = [ph_b.enter_context(nc.sbuf_tensor(f"xres{i}", [128, D], F32))
            for i in range(2)]
    rs_s = ph_b.enter_context(nc.sbuf_tensor("rs_s", [128, D], F32))
    sq_s = q_ax
    mr_s = ph_b.enter_context(nc.sbuf_tensor("mr_s", [128, 1], F32))
    nm_s = ph_b.enter_context(nc.sbuf_tensor("nm_s", [128, 1], F32))
    ssq_s = ph_b.enter_context(nc.sbuf_tensor("ssq_s", [128, 1], F32))
    sd_s = ph_b.enter_context(nc.sbuf_tensor("sd_s", [128, 1], F32))
    rstd_s = ph_b.enter_context(nc.sbuf_tensor("rstd_s", [128, 1], F32))
    b2_s = ph_b.enter_context(nc.sbuf_tensor("b2_s", [128, 1], F32))
    xq_s = ph_b.enter_context(nc.sbuf_tensor("xq_s", [128, D], BF16))
    xqT = es.enter_context(nc.sbuf_tensor("xqT", [128, 8 * T], BF16, side="right"))

    ph_bps = es.enter_context(contextlib.ExitStack())
    ps_o = ph_bps.enter_context(nc.psum_tensor("ps_o", [128, 1024], F32))

    wt(GP, c_dve, last_comb)
    wt(GP, c_reloc[0], 16 * NG * NQB // 2)
    wt(GP, c_reloc[1], 16 * NG * NQB // 2)
    GP.append(lambda e: e.dma_start(
        wo_s[:].rearrange("p (c n) -> p c n", c=8),
        bass.AP(d_w["woT"].tensor, 0, [[D, 128], [128 * D, 8], [1, D]])))
    n_wo = inc(GP, c_dwo, 16)

    if DEBUG_DUMP == 2:
        SP.append(lambda e: e.dma_start(d_dbg_kt[:, 0:512],
                                        dbg_snap[:, 0:512]))
        inc(SP, c_dbg, 16)
        SP.append(lambda e: e.dma_start(d_dbg_qt[:, 0:512],
                                        dbg_snap[:, 512:1024]))
        inc(SP, c_dbg, 16)
    if DEBUG_DUMP:
        wt(SP, c_dve, last_comb)
        for par in range(2):
            wt(SP, c_reloc[par], 16 * NG * NQB // 2)
        SP.append(lambda e: e.dma_start(d_dbg_ut, ut[:]))
        inc(SP, c_dbg, 16)

    res_n = {}
    xqT_n = {}
    last_sq_act = 0
    last_x1_act = 0
    last_xtr = 0
    for q in range(NT):
        if q >= 2:
            wt(SA, c_dve, res_n[q - 2])
        SA.append(lambda e, q=q: e.dma_start(
            xres[q % 2][:], d_x[q * 128:(q + 1) * 128, :]))
        inc(SA, c_dxf[q % 2], 16)
        wt(PE, c_dwo, n_wo)
        if q == 0:
            wt(PE, c_dve, last_comb)
            wt(PE, c_reloc[0], 16 * NG * NQB // 2)
            wt(PE, c_reloc[1], 16 * NG * NQB // 2)
            wt(PE, c_dbo, 16)
        else:
            wt(PE, c_dve, res_n[q - 1])
        for nb in range(2):
            for ic in range(8):
                PE.append(lambda e, q=q, nb=nb, ic=ic: e.matmul(
                    ps_o[:, nb * 512:(nb + 1) * 512],
                    ut[:, ic * T + q * 128:ic * T + (q + 1) * 128],
                    wo_s[:, ic * D + nb * 512:ic * D + (nb + 1) * 512],
                    start=(ic == 0), stop=False))
            PE.append(lambda e, nb=nb: e.matmul(
                ps_o[:, nb * 512:(nb + 1) * 512], ones_all[0:1, 0:128],
                bo_row[:, nb * 512:(nb + 1) * 512], start=False, stop=True))
        n_op = inc(PE, c_pe)
        wt(DV, c_pe, n_op)
        wt(DV, c_dxf[q % 2], (q // 2 + 1) * 16)
        if q >= 1:
            wt(DV, c_act, last_sq_act)
        DV.append(lambda v, q=q: v.tensor_tensor(
            rs_s[:], ps_o[:], xres[q % 2][:], ALU.add))
        res_n[q] = inc(DV, c_dve)
        DV.append(lambda v: v.tensor_reduce(mr_s[:], rs_s[:], axis=AX.X,
                                            op=ALU.add))
        DV.append(lambda v: v.tensor_scalar(nm_s[:], mr_s[:], -1.0 / 1024.0,
                                            None, ALU.mult))
        n_nm = inc(DV, c_dve)
        wt(SA, c_dve, n_nm)
        SA.append(lambda a: a.activation(sq_s[:], rs_s[:], AF.Square,
                                         bias=nm_s[:, 0:1], scale=1.0,
                                         accum_out=ssq_s[:]))
        last_sq_act = inc(SA, c_act)
        SA.append(lambda a: a.activation(sd_s[:], ssq_s[:], AF.Ln,
                                         bias=lc_eps[:, 0:1], scale=1.0 / 1024.0))
        SA.append(lambda a: a.activation(rstd_s[:], sd_s[:], AF.Exp,
                                         bias=0.0, scale=-0.5))
        n_sd = inc(SA, c_act)
        wt(DV, c_act, n_sd)
        DV.append(lambda v: v.tensor_scalar(b2_s[:], mr_s[:], rstd_s[:, 0:1],
                                            -1.0 / 1024.0, ALU.mult, ALU.mult))
        n_rstd = inc(DV, c_dve)
        wt(SA, c_dve, n_rstd)
        SA.append(lambda a, q=q: a.activation(
            acc[:, q * D:(q + 1) * D], rs_s[:], AF.Identity,
            bias=b2_s[:, 0:1], scale=rstd_s[:, 0:1]))
        last_x1_act = inc(SA, c_act)
        wt(DV, c_act, last_x1_act)
        if q >= 1:
            wt(DV, c_pe, last_xtr)
        n_xq = emit_quant(acc[:, q * D:(q + 1) * D], xq_s[:], D)
        wt(PE, c_dve, n_xq)
        if q >= 1:
            wt(PE, c_dve, xqT_n[q - 1])
        for ic in range(8):
            PE.append(lambda e, ic=ic: e.transpose(
                ps_tr[:, ic * 128:(ic + 1) * 128],
                xq_s[:, ic * 128:(ic + 1) * 128], id_bf[:]))
        last_xtr = inc(PE, c_pe)
        wt(DV, c_pe, last_xtr)
        DV.append(lambda v, q=q: v.tensor_copy(
            bass.AP(xqT, q * 128, [[8 * T, 128], [T, 8], [1, 128]]),
            ps_tr[:].rearrange("p (c n) -> p c n", c=8)))
        xqT_n[q] = inc(DV, c_dve)
        emit_wq_loads(2)
        emit_wq_quant(1)

    if DEBUG_DUMP:
        wt(SP, c_dve, xqT_n[NT - 1])
        wt(SP, c_act, last_x1_act)
        SP.append(lambda e: e.dma_start(d_dbg_acc, acc[:]))
        inc(SP, c_dbg, 16)
        SP.append(lambda e: e.dma_start(d_dbg_xqt, xqT[:]))
        inc(SP, c_dbg, 16)

    while wqs["quant"] < len(wq_jobs):
        emit_wq_loads(2)
        emit_wq_quant(2)

    if TRUNC == "p2":
        trunc_finish()
        return nc

    ph_b.close()
    ph_bps.close()
    ut_scope.close()

    ph_f = es.enter_context(contextlib.ExitStack())
    wgT = [ph_f.enter_context(nc.sbuf_tensor(f"wgT{i}", [128, 8 * 512], BF16))
           for i in range(2)]
    wuT = [ph_f.enter_context(nc.sbuf_tensor(f"wuT{i}", [128, 8 * 512], BF16))
           for i in range(2)]
    wdT = [ph_f.enter_context(nc.sbuf_tensor(f"wdT{i}", [128, 4 * 1024], BF16))
           for i in range(2)]
    gs_s = ph_f.enter_context(nc.sbuf_tensor("gs_s", [128, 512], F32))
    h_s = ph_f.enter_context(nc.sbuf_tensor("h_s", [128, 512], F32))
    hq_s = ph_f.enter_context(nc.sbuf_tensor("hq_s", [128, 512], BF16))
    hqT_s = ph_f.enter_context(nc.sbuf_tensor("hqT_s", [128, 512], BF16))
    y_s = ph_f.enter_context(nc.sbuf_tensor("y_s", [128, D], F32))
    mr2 = ph_f.enter_context(nc.sbuf_tensor("mr2", [128, 1], F32))
    nm2 = ph_f.enter_context(nc.sbuf_tensor("nm2", [128, 1], F32))
    ssq2 = ph_f.enter_context(nc.sbuf_tensor("ssq2", [128, 1], F32))
    sd2 = ph_f.enter_context(nc.sbuf_tensor("sd2", [128, 1], F32))
    rstd2 = ph_f.enter_context(nc.sbuf_tensor("rstd2", [128, 1], F32))
    b22 = ph_f.enter_context(nc.sbuf_tensor("b22", [128, 1], F32))
    sq2 = q_ax

    ph_fps = es.enter_context(contextlib.ExitStack())
    ps_gate = ph_fps.enter_context(nc.psum_tensor("ps_gate", [128, 512], F32))
    ps_up = ph_fps.enter_context(nc.psum_tensor("ps_up", [128, 512], F32))
    ps_dn = ph_fps.enter_context(nc.psum_tensor("ps_dn", [128, 1024], F32))

    n_scr0 = 16 * ((len(wq_jobs) + 1) // 2)
    n_scr1 = 16 * (len(wq_jobs) // 2)

    def emit_slab_loads(s, buf, pe_free=0, act_free=0, dve_free=0):
        ns = FFN_SLABS[s]
        wt(SP, c_dsw[0], n_scr0)
        wt(SP, c_dsw[1], n_scr1)
        if pe_free:
            wt(SP, c_pe, pe_free)
        if act_free:
            wt(SP, c_act, act_free)
        if dve_free:
            wt(SP, c_dve, dve_free)
        for hc in range(8):
            SP.append(lambda e, s=s, hc=hc, ns=ns, b=wgT[buf]: e.dma_start(
                b[:, hc * 512:hc * 512 + ns],
                d_sgq[s * 512:s * 512 + ns, hc * 128:(hc + 1) * 128],
                transpose=True))
            inc(SP, c_dst[buf], 16)
        for hc in range(8):
            SP.append(lambda e, s=s, hc=hc, ns=ns, b=wuT[buf]: e.dma_start(
                b[:, hc * 512:hc * 512 + ns],
                d_suq[s * 512:s * 512 + ns, hc * 128:(hc + 1) * 128],
                transpose=True))
            inc(SP, c_dst[buf], 16)
        for icc in range(ns // 128):
            SP.append(lambda e, s=s, icc=icc, b=wdT[buf]: e.dma_start(
                b[:, icc * 1024:(icc + 1) * 1024],
                d_sdq[:, s * 512 + icc * 128:s * 512 + (icc + 1) * 128],
                transpose=True))
            inc(SP, c_dst[buf], 16)
        return c_dst[buf].n

    slab_ready = [0] * len(FFN_SLABS)
    slab_ready[0] = emit_slab_loads(0, 0, pe_free=c_pe.n, act_free=c_act.n,
                                    dve_free=c_dve.n)
    slab_last_pe = [0] * len(FFN_SLABS)
    if DEBUG_DUMP:
        for ch in range(FI // 128):
            SP.append(lambda e, ch=ch: e.dma_start(
                wfq_out[ch % 2][:], d_sgq[ch * 128:(ch + 1) * 128, :]))
            inc(SP, c_dbg, 16)
            wt(SP, c_dbg, c_dbg.n)
            SP.append(lambda e, ch=ch: e.dma_start(
                d_dbg_wgq[ch * 128:(ch + 1) * 128, :], wfq_out[ch % 2][:]))
            inc(SP, c_dbg, 16)
            wt(SP, c_dbg, c_dbg.n)
        wt(SP, c_dst[0], slab_ready[0])
        SP.append(lambda e: e.dma_start(d_dbg_wgt, wgT[0][:]))
        inc(SP, c_dbg, 16)
        SP.append(lambda e: e.dma_start(d_dbg_wdt, wdT[0][:]))
        inc(SP, c_dbg, 16)

    acc_fin = {}
    last_acc = 0
    last_silu = 0
    last_hmul = 0
    last_htr = 0
    last_dn = 0
    for s, ns in enumerate(FFN_SLABS):
        nic = ns // 128
        buf = s % 2
        if s + 1 < len(FFN_SLABS):
            slab_ready[s + 1] = emit_slab_loads(
                s + 1, (s + 1) % 2,
                pe_free=slab_last_pe[s - 1] if s >= 1 else 0)
        for tt in range(NT):
            wt(PE, c_dst[buf], slab_ready[s])
            if s == 0:
                wt(PE, c_dve, xqT_n[tt])
            if last_silu:
                wt(PE, c_act, last_silu)
            if last_hmul:
                wt(PE, c_dve, last_hmul)
            for wsel2, bank in ((wgT[buf], ps_gate), (wuT[buf], ps_up)):
                for ic in range(8):
                    PE.append(lambda e, w=wsel2, ic=ic, tt=tt, b=bank, ns=ns:
                              e.matmul(
                        b[:, 0:ns],
                        xqT[:, ic * T + tt * 128:ic * T + (tt + 1) * 128],
                        w[:, ic * 512:ic * 512 + ns],
                        start=(ic == 0), stop=(ic == 7)))
            n_gu = inc(PE, c_pe)
            wt(SA, c_pe, n_gu)
            if last_hmul:
                wt(SA, c_dve, last_hmul)
            SA.append(lambda a, ns=ns: a.activation(
                gs_s[:, 0:ns], ps_gate[:, 0:ns], AF.Silu))
            last_silu = inc(SA, c_act)
            wt(DV, c_act, last_silu)
            DV.append(lambda v, ns=ns: v.tensor_tensor(
                h_s[:, 0:ns], gs_s[:, 0:ns], ps_up[:, 0:ns], ALU.mult))
            last_hmul = inc(DV, c_dve)
            if last_htr:
                wt(DV, c_pe, last_htr)
            n_hq = emit_quant(h_s[:, 0:ns], hq_s[:, 0:ns], ns)
            wt(PE, c_dve, n_hq)
            for icc in range(nic):
                PE.append(lambda e, icc=icc: e.transpose(
                    ps_tr[:, icc * 128:(icc + 1) * 128],
                    hq_s[:, icc * 128:(icc + 1) * 128], id_bf[:]))
            last_htr = inc(PE, c_pe)
            wt(DV, c_pe, last_htr)
            if last_dn:
                wt(DV, c_pe, last_dn)
            DV.append(lambda v, nic=nic: v.tensor_copy(
                hqT_s[:, 0:nic * 128], ps_tr[:, 0:nic * 128]))
            n_hqT = inc(DV, c_dve)
            wt(PE, c_dve, n_hqT)
            if last_acc:
                wt(PE, c_dve, last_acc)
            for icc in range(nic):
                for nb in range(2):
                    PE.append(lambda e, icc=icc, nb=nb, b=wdT[buf], nic=nic:
                              e.matmul(
                        ps_dn[:, nb * 512:(nb + 1) * 512],
                        hqT_s[:, icc * 128:(icc + 1) * 128],
                        b[:, icc * 1024 + nb * 512:icc * 1024 + (nb + 1) * 512],
                        start=(icc == 0), stop=(icc == nic - 1)))
            last_dn = inc(PE, c_pe)
            slab_last_pe[s] = last_dn
            wt(DV, c_pe, last_dn)
            DV.append(lambda v, tt=tt: v.tensor_tensor(
                acc[:, tt * D:(tt + 1) * D], acc[:, tt * D:(tt + 1) * D],
                ps_dn[:], ALU.add))
            last_acc = inc(DV, c_dve)
            if s == len(FFN_SLABS) - 1:
                acc_fin[tt] = last_acc

    last_sq2 = 0
    for q in range(NT):
        wt(DV, c_dve, acc_fin[q])
        if q >= 1:
            wt(DV, c_act, last_sq2)
        src = acc[:, q * D:(q + 1) * D]
        DV.append(lambda v, s=src: v.tensor_reduce(mr2[:], s, axis=AX.X,
                                                   op=ALU.add))
        DV.append(lambda v: v.tensor_scalar(nm2[:], mr2[:], -1.0 / 1024.0,
                                            None, ALU.mult))
        n_nm = inc(DV, c_dve)
        wt(SA, c_dve, n_nm)
        SA.append(lambda a, s=src: a.activation(sq2[:], s, AF.Square,
                                                bias=nm2[:, 0:1], scale=1.0,
                                                accum_out=ssq2[:]))
        last_sq2 = inc(SA, c_act)
        SA.append(lambda a: a.activation(sd2[:], ssq2[:], AF.Ln,
                                         bias=lc_eps[:, 0:1], scale=1.0 / 1024.0))
        SA.append(lambda a: a.activation(rstd2[:], sd2[:], AF.Exp,
                                         bias=0.0, scale=-0.5))
        n_sd = inc(SA, c_act)
        wt(DV, c_act, n_sd)
        DV.append(lambda v: v.tensor_scalar(b22[:], mr2[:], rstd2[:, 0:1],
                                            -1.0 / 1024.0, ALU.mult, ALU.mult))
        n_rstd = inc(DV, c_dve)
        wt(SA, c_dve, n_rstd)
        if q >= 1:
            wt(SA, c_dout, 16 * q)
        SA.append(lambda a, s=src: a.activation(y_s[:], s, AF.Identity,
                                                bias=b22[:, 0:1],
                                                scale=rstd2[:, 0:1]))
        n_y = inc(SA, c_act)
        wt(SP, c_act, n_y)
        SP.append(lambda e, q=q: e.dma_start(
            d_out[q * 128:(q + 1) * 128, :], y_s[:]))
        inc(SP, c_dout, 16)

    wt(SP, c_dout, 16 * NT)

    with nc.Block() as blk:
        @blk.sync
        def _(e):
            for f in SP:
                f(e)

        @blk.gpsimd
        def _(e):
            for f in GP:
                f(e)

        @blk.tensor
        def _(e):
            for f in PE:
                f(e)

        @blk.scalar
        def _(e):
            for f in SA:
                f(e)

        @blk.vector
        def _(e):
            for f in DV:
                f(e)

    es.close()
    return nc


def kernel(L0_attn_x, L0_attn_Wq, L0_attn_bq, L0_attn_Wk, L0_attn_bk,
           L0_attn_Wv, L0_attn_bv, L0_attn_Wo, L0_attn_bo,
           L0_ffn_w_gate, L0_ffn_w_up, L0_ffn_w_down):
    x = np.ascontiguousarray(np.asarray(L0_attn_x, dtype=np.float32))
    wqT = np.ascontiguousarray(np.asarray(L0_attn_Wq, np.float32).T)
    wkT = np.ascontiguousarray(np.asarray(L0_attn_Wk, np.float32).T)
    wvT = np.ascontiguousarray(np.asarray(L0_attn_Wv, np.float32).T)
    woT = np.ascontiguousarray(np.asarray(L0_attn_Wo, np.float32).T)
    ident = np.eye(128, dtype=np.float32).astype(ml_dtypes.bfloat16)

    nc = build_program()

    lconsts = [build_lband_consts(0), build_lband_consts(1)]
    in_maps = []
    for c in range(8):
        b, half = c // 2, c % 2
        xb = x[b]
        if half == 0:
            x_rot = xb
        else:
            x_rot = np.ascontiguousarray(np.concatenate([xb[T:], xb[:T]], 0))
        m = {
            "xb": x_rot, "wqT": wqT, "wkT": wkT, "wvT": wvT, "woT": woT,
            "bq": np.ascontiguousarray(np.asarray(L0_attn_bq, np.float32)),
            "bk": np.ascontiguousarray(np.asarray(L0_attn_bk, np.float32)),
            "bv": np.ascontiguousarray(np.asarray(L0_attn_bv, np.float32)),
            "bo": np.ascontiguousarray(np.asarray(L0_attn_bo, np.float32)),
            "wg": np.ascontiguousarray(np.asarray(L0_ffn_w_gate, np.float32)),
            "wu": np.ascontiguousarray(np.asarray(L0_ffn_w_up, np.float32)),
            "wd": np.ascontiguousarray(np.asarray(L0_ffn_w_down, np.float32)),
            "ident": ident,
        }
        for k, v in lconsts[half].items():
            m["L_" + k] = v
        in_maps.append(m)

    res = run_bass_kernel_spmd(nc, in_maps, core_ids=list(range(8)))
    out = np.empty((B, S, D), np.float32)
    for c in range(8):
        b, half = c // 2, c % 2
        out[b, half * T:(half + 1) * T] = res.results[c]["out"]
    return out


# version 03 (final)
# speedup vs baseline: 1.4724x; 1.0818x over previous
import contextlib
import numpy as np
import ml_dtypes

import concourse.bass as bass
import concourse.mybir as mybir
from concourse.bass_utils import run_bass_kernel_spmd

F32 = mybir.dt.float32
BF16 = mybir.dt.bfloat16
AF = mybir.ActivationFunctionType
ALU = mybir.AluOpType
AX = mybir.AxisListType

D = 1024
S = 4096
B = 4
T = 2048
NG = 8
NQB = 4
NKC = 32
NT = 16
NCHUNK = 32
FI = 2816
QUANT_THR = [0.25, 0.75, 1.25, 1.75, 2.5, 3.5, 5.0]
QUANT_W2 = [1.0, 1.0, 1.0, 1.0, 2.0, 2.0, 4.0]
FFN_SLABS = [512, 512, 512, 512, 512, 256]
TRUNC = None
NOWQ = False
NOX = False
DELAY_WQ = False
DEBUG_DUMP = False
LN06 = float(np.float32(np.log(0.6)))
INV06 = float(np.float32(1.0))


class Ctr:
    def __init__(self, es, nc, name):
        self.h = es.enter_context(nc.semaphore(name))
        self.n = 0


class Tag:

    def __init__(self, lst):
        self.lst = lst
        self.idx = len(lst) - 1

    def then_inc(self, h, k):
        f = self.lst[self.idx]

        def g(e, f=f, h=h, k=k):
            r = f(e)
            r.then_inc(h, k)
            return r

        self.lst[self.idx] = g


def inc(stream, ctr, k=1):
    Tag(stream).then_inc(ctr.h, k)
    ctr.n += k
    return ctr.n


def wt(stream, ctr, val=None):
    v = ctr.n if val is None else val
    if v > 0:
        stream.append(lambda e, h=ctr.h, vv=v: e.wait_ge(h, vv))


def build_lband_consts(half):
    w = np.array([np.exp(-(d * d) / (2.0 * 0.4 ** 2)) for d in range(-3, 4)])

    def l04(qo, jo):
        if jo < 0 or jo >= S or abs(qo - jo) > 3:
            return 0.0
        z = sum(w[dd + 3] for dd in range(-3, 4) if 0 <= qo + dd < S)
        return 0.4 * w[(jo - qo) + 3] / (z + 1e-10)

    q0 = half * T

    def orig(r):
        return (q0 + r) % S

    def band(dlt):
        if abs(dlt) > 3:
            return 0.0
        return 0.4 * w[-dlt + 3] / (w.sum() + 1e-10)

    pp, uu = np.meshgrid(np.arange(128), np.arange(896), indexing="ij")
    lsh = np.vectorize(band)(uu - 384 - pp)

    def main_tile(qb, jl):
        t = np.zeros((128, 512))
        for p in range(128):
            jr = (qb * 4 + jl) * 128 + p
            lo = max(0, p + 128 * jl * 0 + (jl * 128 + 0) * 0)
            for c in range(max(0, jl * 128 + p - 3), min(512, jl * 128 + p + 4)):
                t[p, c] = l04(orig(qb * 512 + c), orig(jr))
        return t

    lm_first = main_tile(0, 0)
    lm_last = main_tile(3, 3)

    def fringe(qb, pre):
        t = np.zeros((128, 128))
        if pre:
            jc = (qb * 4 - 1) % NCHUNK
            for p in range(125, 128):
                for c in range(0, 3):
                    t[p, c] = l04(orig(qb * 512 + c), orig(jc * 128 + p))
        else:
            jc = (qb * 4 + 4) % NCHUNK
            for p in range(0, 3):
                for c in range(125, 128):
                    t[p, c] = l04(orig(qb * 512 + 384 + c), orig(jc * 128 + p))
        return t

    out = {
        "lsh": lsh,
        "lm_first": lm_first, "lm_last": lm_last,
        "lpre0": fringe(0, True), "lpost3": fringe(3, False),
        "lpre_sh": fringe(1, True), "lpost_sh": fringe(1, False),
    }
    return {k: np.ascontiguousarray(v, np.float32) for k, v in out.items()}


def build_program():
    nc = bass.Bass("TRN2", target_bir_lowering=False, debug=False,
                   detect_race_conditions=False)

    d_x = nc.dram_tensor("xb", [S, D], F32, kind="ExternalInput").ap()
    d_w = {}
    for nm in ["wqT", "wkT", "wvT", "woT"]:
        d_w[nm] = nc.dram_tensor(nm, [D, D], F32, kind="ExternalInput").ap()
    d_b = {}
    for nm in ["bq", "bk", "bv", "bo"]:
        d_b[nm] = nc.dram_tensor(nm, [D], F32, kind="ExternalInput").ap()
    d_wg = nc.dram_tensor("wg", [FI, D], F32, kind="ExternalInput").ap()
    d_wu = nc.dram_tensor("wu", [FI, D], F32, kind="ExternalInput").ap()
    d_wd = nc.dram_tensor("wd", [D, FI], F32, kind="ExternalInput").ap()
    d_ident = nc.dram_tensor("ident", [128, 128], BF16, kind="ExternalInput").ap()
    lshapes = [("lsh", 896), ("lm_first", 512), ("lm_last", 512),
               ("lpre0", 128), ("lpost3", 128), ("lpre_sh", 128),
               ("lpost_sh", 128)]
    d_l = {nm: nc.dram_tensor("L_" + nm, [128, w], F32,
                              kind="ExternalInput").ap()
           for nm, w in lshapes}
    d_out = nc.dram_tensor("out", [T, D], F32, kind="ExternalOutput").ap()
    if DEBUG_DUMP == 2:
        d_dbg_kt = nc.dram_tensor("dbg_kt", [128, 512], BF16,
                                  kind="ExternalOutput").ap()
        d_dbg_qt = nc.dram_tensor("dbg_qt", [128, 512], BF16,
                                  kind="ExternalOutput").ap()
    if DEBUG_DUMP:
        d_dbg_ut = nc.dram_tensor("dbg_ut", [128, 8 * T], BF16,
                                  kind="ExternalOutput").ap()
        d_dbg_acc = nc.dram_tensor("dbg_acc", [128, NT * D], F32,
                                   kind="ExternalOutput").ap()
        d_dbg_xqt = nc.dram_tensor("dbg_xqt", [128, 8 * T], BF16,
                                   kind="ExternalOutput").ap()
        d_dbg_wgq = nc.dram_tensor("dbg_wgq", [FI, D], BF16,
                                   kind="ExternalOutput").ap()
        d_dbg_wgt = nc.dram_tensor("dbg_wgt", [128, 8 * 512], BF16,
                                   kind="ExternalOutput").ap()
        d_dbg_wdt = nc.dram_tensor("dbg_wdt", [128, 4 * 1024], BF16,
                                   kind="ExternalOutput").ap()
    d_sgq = nc.dram_tensor("scr_wgq", [FI, D], BF16, kind="Internal").ap()
    d_suq = nc.dram_tensor("scr_wuq", [FI, D], BF16, kind="Internal").ap()
    d_sdq = nc.dram_tensor("scr_wdq", [D, FI], BF16, kind="Internal").ap()

    es = contextlib.ExitStack()

    PE, SA, DV, SP, GP = [], [], [], [], []

    c_dx = [Ctr(es, nc, "dx0"), Ctr(es, nc, "dx1")]
    c_dxf = [Ctr(es, nc, "dxf0"), Ctr(es, nc, "dxf1")]
    c_dw = Ctr(es, nc, "dw")
    c_dwg = [Ctr(es, nc, "dwg0"), Ctr(es, nc, "dwg1")]
    c_dbo = Ctr(es, nc, "dbo")
    c_dbg = Ctr(es, nc, "dbg")
    c_dwo = Ctr(es, nc, "dwo")
    c_dfw = [Ctr(es, nc, "dfw0"), Ctr(es, nc, "dfw1")]
    c_dsw = [Ctr(es, nc, "dsw0"), Ctr(es, nc, "dsw1")]
    c_dst = [Ctr(es, nc, "dst0"), Ctr(es, nc, "dst1")]
    c_dout = Ctr(es, nc, "dout")
    c_reloc = [Ctr(es, nc, "creloc0"), Ctr(es, nc, "creloc1")]
    c_pe = Ctr(es, nc, "cpe")
    c_act = Ctr(es, nc, "cact")
    c_dve = Ctr(es, nc, "cdve")
    c_exp = Ctr(es, nc, "cexp")

    id_bf = es.enter_context(nc.sbuf_tensor("id_bf", [128, 128], BF16))
    ones_all = es.enter_context(nc.sbuf_tensor("ones_all", [128, 128], BF16))
    b_qkv = es.enter_context(nc.sbuf_tensor("b_qkv", [128, 24], F32))
    bo_row = es.enter_context(nc.sbuf_tensor("bo_row", [1, D], BF16))
    r6 = es.enter_context(nc.sbuf_tensor("r6", [128, 1], F32))
    lc_ln06 = es.enter_context(nc.sbuf_tensor("lc_ln06", [128, 1], F32))
    lc_eps = es.enter_context(nc.sbuf_tensor("lc_eps", [128, 1], F32))
    lbf = {nm: es.enter_context(nc.sbuf_tensor("lbf_" + nm, [128, w], BF16))
           for nm, w in lshapes}
    q_ax = es.enter_context(nc.sbuf_tensor("q_ax", [128, 1024], F32))
    q_sg = es.enter_context(nc.sbuf_tensor("q_sg", [128, 1024], F32))
    q_c = [es.enter_context(nc.sbuf_tensor(f"q_c{i}", [128, 1024], BF16))
           for i in range(3)]
    q_bm = es.enter_context(nc.sbuf_tensor("q_bm", [128, 64], F32))
    q_s = es.enter_context(nc.sbuf_tensor("q_s", [128, 64], F32))
    q_rs = es.enter_context(nc.sbuf_tensor("q_rs", [128, 64], F32))
    q_sh = es.enter_context(nc.sbuf_tensor("q_sh", [128, 64], F32))

    ps_tr = es.enter_context(nc.psum_tensor("ps_tr", [128, 1024], BF16))
    ut_scope = es.enter_context(contextlib.ExitStack())
    ut = ut_scope.enter_context(nc.sbuf_tensor("ut", [128, 8 * T], BF16))
    wfq_in = [ut_scope.enter_context(nc.sbuf_tensor(f"wfqi{i}", [128, 1024],
                                                    F32)) for i in range(2)]
    wfq_out = [ut_scope.enter_context(nc.sbuf_tensor(f"wfqo{i}", [128, 1024],
                                                     BF16)) for i in range(2)]

    def bc16(t, nblk):
        return bass.AP(t, 0, [[t.shape[1], 128], [1, nblk], [0, 16]])

    def r3d(ap_):
        return ap_.rearrange("p (b s) -> p b s", s=16)

    def emit_quant(src_ap, dst_ap, fd):
        nb = fd // 16
        ax = q_ax[:, 0:fd]
        sg = q_sg[:, 0:fd]
        bm = q_bm[:, 0:nb]
        cb = [q_c[i][:, 0:fd] for i in range(3)]
        DV.append(lambda v, a=src_ap, o=bm: v.tensor_reduce(
            o, a.rearrange("p (b s) -> p b s", s=16), axis=AX.X, op=ALU.max,
            apply_absolute_value=True))
        DV.append(lambda v: v.tensor_scalar(
            q_s[:, 0:64], q_bm[:, 0:64], r6[:, 0:1], 1e-12, ALU.mult, ALU.max))
        DV.append(lambda v: v.reciprocal(q_rs[:, 0:64], q_s[:, 0:64]))
        DV.append(lambda v: v.tensor_scalar(
            q_sh[:, 0:64], q_s[:, 0:64], 0.5, None, ALU.mult))
        DV.append(lambda v, i=src_ap, o=ax: v.tensor_scalar(
            o, i, -1.0, None, ALU.mult))
        DV.append(lambda v, i=src_ap, o=ax: v.tensor_tensor(o, o, i, ALU.max))
        DV.append(lambda v, i=src_ap, o=sg: v.tensor_scalar(
            o, i, 0.0, -2.0, ALU.is_lt, ALU.mult))
        DV.append(lambda v, o=sg: v.tensor_scalar(o, o, 1.0, None, ALU.add))
        DV.append(lambda v, o=ax, n=nb: v.tensor_tensor(
            r3d(o), r3d(o), bc16(q_rs, n), ALU.mult))
        DV.append(lambda v, i=ax, o=cb[0]: v.tensor_scalar(
            o, i, QUANT_THR[0], QUANT_W2[0], ALU.is_gt, ALU.mult))
        DV.append(lambda v, i=ax, o=cb[1]: v.tensor_scalar(
            o, i, QUANT_THR[1], QUANT_W2[1], ALU.is_gt, ALU.mult))
        DV.append(lambda v, a=cb[0], b=cb[1]: v.tensor_tensor(a, a, b, ALU.add))
        for k in (2, 3, 4, 5, 6):
            DV.append(lambda v, i=ax, o=cb[1], k=k: v.tensor_scalar(
                o, i, QUANT_THR[k], QUANT_W2[k], ALU.is_gt, ALU.mult))
            DV.append(lambda v, a=cb[0], b=cb[1]: v.tensor_tensor(
                a, a, b, ALU.add))
        DV.append(lambda v, g=cb[0], o=ax, n=nb: v.tensor_tensor(
            r3d(o), r3d(g), bc16(q_sh, n), ALU.mult))
        DV.append(lambda v, a=ax, b=sg, o=dst_ap: v.tensor_tensor(
            o, a, b, ALU.mult))
        return inc(DV, c_dve)

    ph_a = es.enter_context(contextlib.ExitStack())
    xt = ph_a.enter_context(nc.sbuf_tensor("xt", [128, 8 * S], BF16))
    ph_0 = es.enter_context(contextlib.ExitStack())
    xld = [ph_0.enter_context(nc.sbuf_tensor(f"xld{i}", [128, D], BF16))
           for i in range(2)]
    l32 = {nm: ph_0.enter_context(nc.sbuf_tensor("l32_" + nm, [128, w], F32))
           for nm, w in lshapes}

    SP.append(lambda e: e.dma_start(id_bf[:], d_ident))
    inc(SP, c_dw, 16)
    for nm, _w in lshapes:
        SP.append(lambda e, a=l32[nm], b=d_l[nm]: e.dma_start(a[:], b))
        inc(SP, c_dw, 16)
    for j, nm in enumerate(["bq", "bk", "bv"]):
        SP.append(lambda e, j=j, dt=d_b[nm]: e.dma_start(
            b_qkv[:, j * 8:(j + 1) * 8],
            bass.AP(dt.tensor, 0, [[1, 128], [128, 8]]),
            allow_slow_non_contiguous=True))
        inc(SP, c_dw, 16)
    GP.append(lambda e: e.dma_start(
        bo_row[:], bass.AP(d_b["bo"].tensor, 0, [[D, 1], [1, D]])))
    inc(GP, c_dbo, 16)

    wt(DV, c_dw)
    DV.append(lambda v: v.memset(r6[:], 6.0))
    DV.append(lambda v: v.memset(q_s[:], 1.0))
    DV.append(lambda v: v.memset(q_bm[:], 1.0))
    DV.append(lambda v: v.reciprocal(r6[:], r6[:]))
    DV.append(lambda v: v.memset(ones_all[:], 1.0))
    DV.append(lambda v: v.memset(lc_ln06[:], LN06))
    DV.append(lambda v: v.memset(lc_eps[:], 1e-5))
    DV.append(lambda v: v.tensor_scalar(b_qkv[:, 0:8], b_qkv[:, 0:8], 0.125,
                                        None, ALU.mult))
    for nm, _w in lshapes:
        DV.append(lambda v, a=lbf[nm], b=l32[nm]: v.tensor_copy(a[:], b[:]))
    n_consts = inc(DV, c_dve)

    xt_evac = {}
    for tc in ([] if NOX else range(NCHUNK)):
        if tc >= 2:
            wt(GP, c_dve, xt_evac[tc - 2])
        GP.append(lambda e, tc=tc, b=xld[tc % 2]: e.dma_start(
            b[:], d_x[tc * 128:(tc + 1) * 128, :]))
        inc(GP, c_dx[tc % 2], 16)
        wt(PE, c_dx[tc % 2], (tc // 2 + 1) * 16)
        if tc == 0:
            wt(PE, c_dw)
        if tc >= 1:
            wt(PE, c_dve, xt_evac[tc - 1])
        for ic in range(8):
            PE.append(lambda e, b=xld[tc % 2], ic=ic: e.transpose(
                ps_tr[:, ic * 128:(ic + 1) * 128],
                b[:, ic * 128:(ic + 1) * 128], id_bf[:]))
        n_pe = inc(PE, c_pe)
        wt(DV, c_pe, n_pe)
        DV.append(lambda v, tc=tc: v.tensor_copy(
            bass.AP(xt, tc * 128, [[8 * S, 128], [S, 8], [1, 128]]),
            ps_tr[:].rearrange("p (c n) -> p c n", c=8)))
        xt_evac[tc] = inc(DV, c_dve)

    ph_0.close()

    if NOX:
        for tc in range(NCHUNK):
            xt_evac[tc] = 0

    kt = ph_a.enter_context(nc.sbuf_tensor("kt", [128, S], BF16))
    qt_s = ph_a.enter_context(nc.sbuf_tensor("qt_s", [128, T], BF16))
    vt_st = ph_a.enter_context(nc.sbuf_tensor("vt_st", [128, S], BF16))
    v_s = [ph_a.enter_context(nc.sbuf_tensor(f"v_s{i}", [128, NCHUNK * 130],
                                             BF16)) for i in range(2)]
    pt = [ph_a.enter_context(nc.sbuf_tensor(f"pt{i}", [128, 1024], BF16))
          for i in range(2)]
    lv_s = ph_a.enter_context(nc.sbuf_tensor("lv_s", [64, 2 * NQB * 512], F32))
    u1st = [ph_a.enter_context(nc.sbuf_tensor(f"u1st{i}", [64, 512], BF16))
            for i in range(2)]
    zln = ph_a.enter_context(nc.sbuf_tensor("zln", [128, 512], F32))
    zib = ph_a.enter_context(nc.sbuf_tensor("zib", [128, 1024], BF16))
    tmp_u = ph_a.enter_context(nc.sbuf_tensor("tmp_u", [128, 512], F32))
    wqkv = [ph_a.enter_context(nc.sbuf_tensor(f"wqkv{i}", [128, 3 * 1024],
                                              BF16)) for i in range(2)]
    dbg_snap = q_c[2]


    ph_aps = es.enter_context(contextlib.ExitStack())
    ps_s01 = ph_aps.enter_context(nc.psum_tensor("ps_s01", [128, 1024], F32))
    ps_s23 = ph_aps.enter_context(nc.psum_tensor("ps_s23", [128, 1024], F32))
    ps_u0 = ph_aps.enter_context(nc.psum_tensor("ps_u0", [128, 512], F32))
    ps_u1 = ph_aps.enter_context(nc.psum_tensor("ps_u1", [128, 512], F32))
    ps_g = ph_aps.enter_context(nc.psum_tensor("ps_g", [128, 512], F32))

    for i in range(2):
        DV.append(lambda v, vb=v_s[i]: v.memset(
            bass.AP(vb, 64, [[NCHUNK * 130, 128], [65, 64]]), 1.0))
    DV.append(lambda v: v.memset(zib[:], 0.0))
    n_vones = inc(DV, c_dve)

    wq_jobs = []
    for ch in range(FI // 128):
        wq_jobs.append((d_wg[ch * 128:(ch + 1) * 128, :],
                        d_sgq[ch * 128:(ch + 1) * 128, :], D))
    for ch in range(FI // 128):
        wq_jobs.append((d_wu[ch * 128:(ch + 1) * 128, :],
                        d_suq[ch * 128:(ch + 1) * 128, :], D))
    for ch in range(D // 128):
        for pc in range(4):
            wq_jobs.append((d_wd[ch * 128:(ch + 1) * 128,
                                 pc * 704:(pc + 1) * 704],
                            d_sdq[ch * 128:(ch + 1) * 128,
                                  pc * 704:(pc + 1) * 704], 704))
    wqs = {"load": 0, "quant": 0, "loadn": {}, "qn": {}, "qdn": {}}

    def emit_wq_loads(k):
        for _ in range(k):
            i = wqs["load"]
            if i >= len(wq_jobs):
                return
            if i - 2 >= 0:
                if i - 2 not in wqs["qdn"]:
                    return
                wt(SP, c_dve, wqs["qdn"][i - 2])
            src, _, fd = wq_jobs[i]
            SP.append(lambda e, b=wfq_in[i % 2], s=src, fd=fd: e.dma_start(
                b[:, 0:fd], s))
            wqs["loadn"][i] = inc(SP, c_dfw[i % 2], 16)
            wqs["load"] = i + 1

    def emit_wq_quant(k):
        if NOWQ:
            return
        for _ in range(k):
            i = wqs["quant"]
            if i >= wqs["load"]:
                return
            _, dst, fd = wq_jobs[i]
            wt(DV, c_dfw[i % 2], wqs["loadn"][i])
            if i >= 2:
                wt(DV, c_dsw[i % 2], 16 * ((i - 2) // 2 + 1))
            nq = emit_quant(wfq_in[i % 2][:, 0:fd], wfq_out[i % 2][:, 0:fd], fd)
            wqs["qdn"][i] = nq
            wt(SP, c_dve, nq)
            SP.append(lambda e, b=wfq_out[i % 2], d=dst, fd=fd: e.dma_start(
                d, b[:, 0:fd]))
            wqs["qn"][i] = inc(SP, c_dsw[i % 2], 16)
            wqs["quant"] = i + 1

    def emit_wqkv_load(g):
        if g >= 2:
            wt(GP, c_act, n_proj_done[g - 2])
        for j, nm in enumerate(["wqT", "wkT", "wvT"]):
            GP.append(lambda e, j=j, dt=d_w[nm], g=g: e.dma_start(
                wqkv[g % 2][:, j * 1024:(j + 1) * 1024].rearrange(
                    "p (c n) -> p c n", c=8),
                bass.AP(dt.tensor, g * 128, [[D, 128], [128 * D, 8], [1, 128]])))
            inc(GP, c_dwg[g % 2], 16)
        return c_dwg[g % 2].n

    def l_main_ap(qb, jl):
        if qb == 0 and jl == 0:
            return lbf["lm_first"][:]
        if qb == 3 and jl == 3:
            return lbf["lm_last"][:]
        off = 384 - jl * 128
        return lbf["lsh"][:, off:off + 512]

    n_wqkv = [0] * (NG + 1)
    n_proj_done = [0] * NG
    n_wqkv[0] = emit_wqkv_load(0)
    if not DELAY_WQ:
        emit_wq_loads(2)

    ng_run = NG
    if TRUNC == "p0":
        ng_run = 0
    elif isinstance(TRUNC, str) and TRUNC.startswith("g") and TRUNC[1:].isdigit():
        ng_run = int(TRUNC[1:])

    def emit_block():
        with nc.Block() as blk:
            @blk.sync
            def _(e):
                for f in SP:
                    f(e)

            @blk.gpsimd
            def _(e):
                for f in GP:
                    f(e)

            @blk.tensor
            def _(e):
                for f in PE:
                    f(e)

            @blk.scalar
            def _(e):
                for f in SA:
                    f(e)

            @blk.vector
            def _(e):
                for f in DV:
                    f(e)

    def trunc_finish():
        if TRUNC == "p0":
            while wqs["quant"] < len(wq_jobs):
                emit_wq_loads(2)
                emit_wq_quant(2)
        while wqs["quant"] < wqs["load"]:
            emit_wq_quant(4)
        if DEBUG_DUMP:
            d_dbg_wdq2 = nc.dram_tensor("dbg_wdq", [D, FI], BF16,
                                        kind="ExternalOutput").ap()
            wt(GP, c_dsw[0], 16 * ((len(wq_jobs) + 1) // 2))
            wt(GP, c_dsw[1], 16 * (len(wq_jobs) // 2))
            for ch in range(FI // 128):
                GP.append(lambda e, ch=ch: e.dma_start(
                    wfq_out[0][:], d_sgq[ch * 128:(ch + 1) * 128, :]))
                inc(GP, c_dbo, 16)
                wt(GP, c_dbo, c_dbo.n)
                GP.append(lambda e, ch=ch: e.dma_start(
                    d_dbg_wgq[ch * 128:(ch + 1) * 128, :], wfq_out[0][:]))
                inc(GP, c_dbo, 16)
                wt(GP, c_dbo, c_dbo.n)
            for ch in range(D // 128):
                for pc in range(4):
                    GP.append(lambda e, ch=ch, pc=pc: e.dma_start(
                        wfq_out[0][:, 0:704],
                        d_sdq[ch * 128:(ch + 1) * 128, pc * 704:(pc + 1) * 704]))
                    inc(GP, c_dbo, 16)
                    wt(GP, c_dbo, c_dbo.n)
                    GP.append(lambda e, ch=ch, pc=pc: e.dma_start(
                        d_dbg_wdq2[ch * 128:(ch + 1) * 128,
                                   pc * 704:(pc + 1) * 704],
                        wfq_out[0][:, 0:704]))
                    inc(GP, c_dbo, 16)
                    wt(GP, c_dbo, c_dbo.n)
            wt(SP, c_dbg, c_dbg.n)
        ys_ = wfq_in[0]
        DV.append(lambda v: v.memset(ys_[:], 0.0))
        nd = inc(DV, c_dve)
        wt(SP, c_dve, nd)
        for qq in range(NT):
            SP.append(lambda e, qq=qq: e.dma_start(
                d_out[qq * 128:(qq + 1) * 128, :], ys_[:]))
            inc(SP, c_dout, 16)
        wt(SP, c_dout, 16 * NT)
        emit_block()
        es.close()

    last_comb = n_consts
    last_exp = 0
    v_evac = [0, 0]
    av_done = [0, 0]
    last_av_slot = [0, 0]

    CUT = [False]

    for g in range(ng_run):
        wsel = wqkv[g % 2]
        vsel = v_s[g % 2]
        wt(PE, c_dwg[g % 2], n_wqkv[g])
        if g == 0:
            wt(PE, c_dve, xt_evac[NCHUNK - 1])
            wt(PE, c_dve, n_consts)
        else:
            wt(PE, c_exp, last_exp)
            wt(PE, c_dve, last_comb)
        proj_jobs = ([("k", fb, kt, S) for fb in range(8)]
                     + [("q", fb, qt_s, T) for fb in range(4)]
                     + [("v", fb, vt_st, S) for fb in range(8)])
        pj_evac = []
        for jn, (kind, fb, dst, span) in enumerate(proj_jobs):
            wj = {"q": 0, "k": 1, "v": 2}[kind]
            bank = ps_g[:, 0:512] if jn % 2 == 0 else ps_s01[:, 0:512]
            col = fb * 512
            if jn >= 2:
                wt(PE, c_act, pj_evac[jn - 2])
            for ic in range(8):
                PE.append(lambda e, w=wsel, wj=wj, ic=ic, b=bank, col=col:
                          e.matmul(
                    b, w[:, wj * 1024 + ic * 128:wj * 1024 + (ic + 1) * 128],
                    xt[:, ic * S + col:ic * S + col + 512],
                    start=(ic == 0), stop=(ic == 7)))
            n_mm = inc(PE, c_pe)
            if DEBUG_DUMP == 2 and g == 0 and kind == "q" and fb == 0:
                wt(DV, c_pe, n_mm)
                DV.append(lambda v, b=bank: v.tensor_copy(
                    dbg_snap[:, 512:1024], b))
                n_snapq = inc(DV, c_dve)
                wt(SA, c_dve, n_snapq)
            wt(SA, c_pe, n_mm)
            scale = 0.125 if kind == "q" else 1.0
            bcol = {"q": g, "k": 8 + g, "v": 16 + g}[kind]
            SA.append(lambda a, b=bank, d=dst, col=col, bcol=bcol, scale=scale:
                      a.activation(d[:, col:col + 512], b, AF.Identity,
                                   bias=b_qkv[:, bcol:bcol + 1], scale=scale))
            pj_evac.append(inc(SA, c_act))
        n_proj_all = pj_evac[-1]
        n_proj_done[g] = n_proj_all
        if TRUNC == f"G{g}p":
            trunc_finish()
            return nc
        if g == 0 and TRUNC == "gp":
            trunc_finish()
            return nc

        wt(PE, c_act, n_proj_all)
        for batch in range(4):
            if batch > 0:
                wt(PE, c_dve, v_evac[g % 2])
            elif g > 0:
                wt(PE, c_dve, v_evac[(g - 1) % 2])
            for j in range(8):
                tc = batch * 8 + j
                PE.append(lambda e, tc=tc, j=j: e.transpose(
                    ps_tr[:, j * 128:(j + 1) * 128],
                    vt_st[:, tc * 128:(tc + 1) * 128], id_bf[:]))
            n_tr = inc(PE, c_pe)
            wt(DV, c_pe, n_tr)
            if g == 0 and batch == 0:
                wt(DV, c_dve, n_vones)
            if g >= 2 and batch == 0:
                wt(DV, c_pe, av_done[g % 2])
            DV.append(lambda v, batch=batch, vb=vsel: v.tensor_copy(
                bass.AP(vb, batch * 8 * 130,
                        [[NCHUNK * 130, 128], [130, 8], [65, 2], [1, 64]]),
                ps_tr[:].rearrange("p (c h n) -> p c h n", c=8, h=2)))
            v_evac[g % 2] = inc(DV, c_dve)

        if g == 0 and TRUNC == "gv":
            trunc_finish()
            return nc
        if TRUNC == f"G{g}v":
            trunc_finish()
            return nc
        if g + 1 < NG:
            n_wqkv[g + 1] = emit_wqkv_load(g + 1)
        if not DELAY_WQ:
            emit_wq_loads(3)
        if TRUNC == f"G{g}w":
            trunc_finish()
            return nc

        if DEBUG_DUMP == 2 and g == 0:
            wt(DV, c_act, n_proj_all)
            DV.append(lambda v: v.tensor_copy(dbg_snap[:, 0:512],
                                              kt[:, 1024:1536]))
            DV.append(lambda v: v.tensor_copy(dbg_snap[:, 512:1024],
                                              qt_s[:, 0:512]))
            inc(DV, c_dve)

        wt(PE, c_dve, v_evac[g % 2])
        lv_evacs = []
        for h in range(2):
            for qb in range(NQB):
                bank = [ps_u0, ps_u1][(h * NQB + qb) % 2]
                if len(lv_evacs) >= 2:
                    wt(PE, c_dve, lv_evacs[-2])
                elif g > 0 and len(lv_evacs) == 0:
                    wt(PE, c_dve, last_comb)

                def vstat(vb, jc, h):
                    return vb[:, jc * 130 + h * 65:jc * 130 + h * 65 + 64]
                for jl in range(4):
                    jc = qb * 4 + jl
                    PE.append(lambda e, b=bank, vb=vsel, jc=jc, h=h, qb=qb,
                              jl=jl: e.matmul(
                        b[0:64, :], vstat(vb, jc, h),
                        l_main_ap(qb, jl), start=(jl == 0), stop=False))
                jc_pre = (qb * 4 - 1) % NCHUNK
                pre_t = lbf["lpre0"] if qb == 0 else lbf["lpre_sh"]
                PE.append(lambda e, b=bank, vb=vsel, jc=jc_pre, h=h, t=pre_t:
                          e.matmul(
                    b[0:64, 0:128], vstat(vb, jc, h),
                    t[:], start=False, stop=False))
                jc_post = (qb * 4 + 4) % NCHUNK
                post_t = lbf["lpost3"] if qb == 3 else lbf["lpost_sh"]
                PE.append(lambda e, b=bank, vb=vsel, jc=jc_post, h=h,
                          t=post_t: e.matmul(
                    b[0:64, 384:512], vstat(vb, jc, h),
                    t[:], start=False, stop=True))
                n_l = inc(PE, c_pe)
                wt(DV, c_pe, n_l)
                DV.append(lambda v, b=bank, h=h, qb=qb: v.tensor_copy(
                    lv_s[0:64, (h * NQB + qb) * 512:(h * NQB + qb + 1) * 512],
                    b[0:64, :]))
                lv_evacs.append(inc(DV, c_dve))
        n_lv_all = lv_evacs[-1]

        if not DELAY_WQ:
            emit_wq_quant(3)
        if g == 0 and TRUNC == "gl":
            trunc_finish()
            return nc
        if TRUNC == f"G{g}l":
            trunc_finish()
            return nc

        for qb in range(NQB):
            exp_n = {}
            avn = {}
            def emit_scores(kc):
                nonlocal last_exp
                sb = [ps_s01, ps_s23][kc % 2]
                if kc >= 2:
                    wt(PE, c_exp, exp_n[kc - 2])
                elif qb == 0 and kc < 2:
                    wt(PE, c_dve, n_lv_all)
                if qb > 0 and kc < 2:
                    wt(PE, c_dve, last_comb)
                PE.append(lambda e, sb=sb, kc=kc, qb=qb: e.matmul(
                    sb[:, 0:512], kt[0:64, kc * 128:(kc + 1) * 128],
                    qt_s[0:64, qb * 512:(qb + 1) * 512],
                    start=True, stop=True, tile_position=(0, 0)))
                PE.append(lambda e, sb=sb, kc=kc, qb=qb: e.matmul(
                    sb[:, 512:1024], kt[64:128, kc * 128:(kc + 1) * 128],
                    qt_s[64:128, qb * 512:(qb + 1) * 512],
                    start=True, stop=True, tile_position=(64, 0)))
                n_s = inc(PE, c_pe)
                wt(SA, c_pe, n_s)
                wt(SA, c_pe, last_av_slot[kc % 2])
                SA.append(lambda a, sb=sb, slot=kc % 2: a.activation(
                    pt[slot][:], sb[:], AF.Exp))
                exp_n[kc] = inc(SA, c_exp)
                last_exp = exp_n[kc]

            def emit_av(kc):
                wt(PE, c_exp, exp_n[kc])
                PE.append(lambda e, vb=vsel, kc=kc, slot=kc % 2: e.matmul(
                    ps_u0[0:65, :], vb[:, kc * 130:kc * 130 + 65],
                    pt[slot][:, 0:512], start=(kc == 0), stop=(kc == NKC - 1)))
                PE.append(lambda e, vb=vsel, kc=kc, slot=kc % 2: e.matmul(
                    ps_u1[0:65, :], vb[:, kc * 130 + 65:kc * 130 + 130],
                    pt[slot][:, 512:1024], start=(kc == 0),
                    stop=(kc == NKC - 1)))
                avn[kc] = inc(PE, c_pe)
                last_av_slot[kc % 2] = avn[kc]

            emit_scores(0)
            for kc in range(1, NKC):
                emit_scores(kc)
                emit_av(kc - 1)
                if g == 0 and TRUNC == f"gq{qb}k{kc - 1}":
                    trunc_finish()
                    return nc
            emit_av(NKC - 1)
            av_done[g % 2] = avn[NKC - 1]
            wt(SA, c_pe, avn[NKC - 1])
            for h in range(2):
                ub = [ps_u0, ps_u1][h]
                SA.append(lambda a, ub=ub: a.activation(
                    zln[64:65, 0:512], ub[64:65, :], AF.Ln))
                inc(SA, c_act)
                SA.append(lambda a, h=h: a.activation(
                    zib[64:65, h * 512:(h + 1) * 512], zln[64:65, 0:512],
                    AF.Exp, bias=lc_ln06[64:65, 0:1], scale=-1.0))
                inc(SA, c_act)
            n_zi = c_act.n
            wt(PE, c_act, n_zi)
            wt(PE, c_exp, last_exp)
            for h in range(2):
                PE.append(lambda e, h=h: e.matmul(
                    ps_s01[0:64, h * 512:(h + 1) * 512], ones_all[64:65, 0:64],
                    zib[64:65, h * 512:(h + 1) * 512], start=True, stop=True))
            n_zb = inc(PE, c_pe)
            wt(DV, c_pe, n_zb)
            for h in range(2):
                ub = [ps_u0, ps_u1][h]
                zb_ap = ps_s01[0:64, h * 512:(h + 1) * 512]
                DV.append(lambda v, ub=ub: v.tensor_copy(
                    tmp_u[0:64, :], ub[0:64, :]))
                DV.append(lambda v, zb=zb_ap: v.tensor_tensor(
                    tmp_u[0:64, :], tmp_u[0:64, :], zb, ALU.mult))
                if h == 0:
                    DV.append(lambda v, qb=qb, g=g: v.tensor_tensor(
                        ut[0:64, g * T + qb * 512:g * T + (qb + 1) * 512],
                        tmp_u[0:64, :],
                        lv_s[0:64, qb * 512:(qb + 1) * 512], ALU.add))
                else:
                    sidx = (g * NQB + qb) % 2
                    if g * NQB + qb >= 2:
                        wt(DV, c_reloc[sidx], 16 * ((g * NQB + qb) // 2))
                    DV.append(lambda v, qb=qb, sidx=sidx: v.tensor_tensor(
                        u1st[sidx][:, :], tmp_u[0:64, :],
                        lv_s[0:64, (NQB + qb) * 512:(NQB + qb + 1) * 512],
                        ALU.add))
            last_comb = inc(DV, c_dve)
            if g == 0 and TRUNC == f"gq{qb}nr":
                trunc_finish()
                return nc
            sidx = (g * NQB + qb) % 2
            wt(SA, c_dve, last_comb)
            SA.append(lambda a, qb=qb, g=g, sidx=sidx: a.dma_start(
                ut[64:128, g * T + qb * 512:g * T + (qb + 1) * 512],
                u1st[sidx][:, :]))
            inc(SA, c_reloc[sidx], 16)
            if g == 0 and TRUNC == f"gq{qb}":
                trunc_finish()
                return nc
            if TRUNC == f"G{g}q{qb}":
                trunc_finish()
                return nc
            if not DELAY_WQ:
                emit_wq_quant(1)
                emit_wq_loads(2)
        if TRUNC == f"G{g}":
            trunc_finish()
            return nc

    if TRUNC is not None and (TRUNC == "p0" or TRUNC.startswith("g")):
        trunc_finish()
        return nc

    ph_a.close()
    ph_aps.close()

    ph_b = es.enter_context(contextlib.ExitStack())
    wo_s = ph_b.enter_context(nc.sbuf_tensor("wo_s", [128, 8 * D], BF16))
    acc = es.enter_context(nc.sbuf_tensor("acc", [128, NT * D], F32, side="right"))
    xres = [ph_b.enter_context(nc.sbuf_tensor(f"xres{i}", [128, D], F32))
            for i in range(2)]
    rs_s = ph_b.enter_context(nc.sbuf_tensor("rs_s", [128, D], F32))
    sq_s = q_ax
    mr_s = ph_b.enter_context(nc.sbuf_tensor("mr_s", [128, 1], F32))
    nm_s = ph_b.enter_context(nc.sbuf_tensor("nm_s", [128, 1], F32))
    ssq_s = ph_b.enter_context(nc.sbuf_tensor("ssq_s", [128, 1], F32))
    sd_s = ph_b.enter_context(nc.sbuf_tensor("sd_s", [128, 1], F32))
    rstd_s = ph_b.enter_context(nc.sbuf_tensor("rstd_s", [128, 1], F32))
    b2_s = ph_b.enter_context(nc.sbuf_tensor("b2_s", [128, 1], F32))
    xq_s = ph_b.enter_context(nc.sbuf_tensor("xq_s", [128, D], BF16))
    xqT = es.enter_context(nc.sbuf_tensor("xqT", [128, 8 * T], BF16, side="right"))

    ph_bps = es.enter_context(contextlib.ExitStack())
    ps_o = ph_bps.enter_context(nc.psum_tensor("ps_o", [128, 1024], F32))

    wt(GP, c_dve, last_comb)
    wt(GP, c_reloc[0], 16 * NG * NQB // 2)
    wt(GP, c_reloc[1], 16 * NG * NQB // 2)
    GP.append(lambda e: e.dma_start(
        wo_s[:].rearrange("p (c n) -> p c n", c=8),
        bass.AP(d_w["woT"].tensor, 0, [[D, 128], [128 * D, 8], [1, D]])))
    n_wo = inc(GP, c_dwo, 16)

    if DEBUG_DUMP == 2:
        SP.append(lambda e: e.dma_start(d_dbg_kt[:, 0:512],
                                        dbg_snap[:, 0:512]))
        inc(SP, c_dbg, 16)
        SP.append(lambda e: e.dma_start(d_dbg_qt[:, 0:512],
                                        dbg_snap[:, 512:1024]))
        inc(SP, c_dbg, 16)
    if DEBUG_DUMP:
        wt(SP, c_dve, last_comb)
        for par in range(2):
            wt(SP, c_reloc[par], 16 * NG * NQB // 2)
        SP.append(lambda e: e.dma_start(d_dbg_ut, ut[:]))
        inc(SP, c_dbg, 16)

    res_n = {}
    xqT_n = {}
    last_sq_act = 0
    last_x1_act = 0
    last_xtr = 0
    for q in range(NT):
        if q >= 2:
            wt(SA, c_dve, res_n[q - 2])
        SA.append(lambda e, q=q: e.dma_start(
            xres[q % 2][:], d_x[q * 128:(q + 1) * 128, :]))
        inc(SA, c_dxf[q % 2], 16)
        wt(PE, c_dwo, n_wo)
        if q == 0:
            wt(PE, c_dve, last_comb)
            wt(PE, c_reloc[0], 16 * NG * NQB // 2)
            wt(PE, c_reloc[1], 16 * NG * NQB // 2)
            wt(PE, c_dbo, 16)
        else:
            wt(PE, c_dve, res_n[q - 1])
        for nb in range(2):
            for ic in range(8):
                PE.append(lambda e, q=q, nb=nb, ic=ic: e.matmul(
                    ps_o[:, nb * 512:(nb + 1) * 512],
                    ut[:, ic * T + q * 128:ic * T + (q + 1) * 128],
                    wo_s[:, ic * D + nb * 512:ic * D + (nb + 1) * 512],
                    start=(ic == 0), stop=False))
            PE.append(lambda e, nb=nb: e.matmul(
                ps_o[:, nb * 512:(nb + 1) * 512], ones_all[0:1, 0:128],
                bo_row[:, nb * 512:(nb + 1) * 512], start=False, stop=True))
        n_op = inc(PE, c_pe)
        wt(DV, c_pe, n_op)
        wt(DV, c_dxf[q % 2], (q // 2 + 1) * 16)
        if q >= 1:
            wt(DV, c_act, last_sq_act)
        DV.append(lambda v, q=q: v.tensor_tensor(
            rs_s[:], ps_o[:], xres[q % 2][:], ALU.add))
        res_n[q] = inc(DV, c_dve)
        DV.append(lambda v: v.tensor_reduce(mr_s[:], rs_s[:], axis=AX.X,
                                            op=ALU.add))
        DV.append(lambda v: v.tensor_scalar(nm_s[:], mr_s[:], -1.0 / 1024.0,
                                            None, ALU.mult))
        n_nm = inc(DV, c_dve)
        wt(SA, c_dve, n_nm)
        SA.append(lambda a: a.activation(sq_s[:], rs_s[:], AF.Square,
                                         bias=nm_s[:, 0:1], scale=1.0,
                                         accum_out=ssq_s[:]))
        last_sq_act = inc(SA, c_act)
        SA.append(lambda a: a.activation(sd_s[:], ssq_s[:], AF.Ln,
                                         bias=lc_eps[:, 0:1], scale=1.0 / 1024.0))
        SA.append(lambda a: a.activation(rstd_s[:], sd_s[:], AF.Exp,
                                         bias=0.0, scale=-0.5))
        n_sd = inc(SA, c_act)
        wt(DV, c_act, n_sd)
        DV.append(lambda v: v.tensor_scalar(b2_s[:], mr_s[:], rstd_s[:, 0:1],
                                            -1.0 / 1024.0, ALU.mult, ALU.mult))
        n_rstd = inc(DV, c_dve)
        wt(SA, c_dve, n_rstd)
        SA.append(lambda a, q=q: a.activation(
            acc[:, q * D:(q + 1) * D], rs_s[:], AF.Identity,
            bias=b2_s[:, 0:1], scale=rstd_s[:, 0:1]))
        last_x1_act = inc(SA, c_act)
        wt(DV, c_act, last_x1_act)
        if q >= 1:
            wt(DV, c_pe, last_xtr)
        n_xq = emit_quant(acc[:, q * D:(q + 1) * D], xq_s[:], D)
        wt(PE, c_dve, n_xq)
        if q >= 1:
            wt(PE, c_dve, xqT_n[q - 1])
        for ic in range(8):
            PE.append(lambda e, ic=ic: e.transpose(
                ps_tr[:, ic * 128:(ic + 1) * 128],
                xq_s[:, ic * 128:(ic + 1) * 128], id_bf[:]))
        last_xtr = inc(PE, c_pe)
        wt(DV, c_pe, last_xtr)
        DV.append(lambda v, q=q: v.tensor_copy(
            bass.AP(xqT, q * 128, [[8 * T, 128], [T, 8], [1, 128]]),
            ps_tr[:].rearrange("p (c n) -> p c n", c=8)))
        xqT_n[q] = inc(DV, c_dve)
        emit_wq_loads(2)
        emit_wq_quant(1)

    if DEBUG_DUMP:
        wt(SP, c_dve, xqT_n[NT - 1])
        wt(SP, c_act, last_x1_act)
        SP.append(lambda e: e.dma_start(d_dbg_acc, acc[:]))
        inc(SP, c_dbg, 16)
        SP.append(lambda e: e.dma_start(d_dbg_xqt, xqT[:]))
        inc(SP, c_dbg, 16)

    while wqs["quant"] < len(wq_jobs):
        emit_wq_loads(2)
        emit_wq_quant(2)

    if TRUNC == "p2":
        trunc_finish()
        return nc

    ph_b.close()
    ph_bps.close()
    ut_scope.close()

    ph_f = es.enter_context(contextlib.ExitStack())
    wgT = [ph_f.enter_context(nc.sbuf_tensor(f"wgT{i}", [128, 8 * 512], BF16))
           for i in range(2)]
    wuT = [ph_f.enter_context(nc.sbuf_tensor(f"wuT{i}", [128, 8 * 512], BF16))
           for i in range(2)]
    wdT = [ph_f.enter_context(nc.sbuf_tensor(f"wdT{i}", [128, 4 * 1024], BF16))
           for i in range(2)]
    gs_s = ph_f.enter_context(nc.sbuf_tensor("gs_s", [128, 512], F32))
    h_s = ph_f.enter_context(nc.sbuf_tensor("h_s", [128, 512], F32))
    hq_s = ph_f.enter_context(nc.sbuf_tensor("hq_s", [128, 512], BF16))
    hqT_s = ph_f.enter_context(nc.sbuf_tensor("hqT_s", [128, 512], BF16))
    y_s = ph_f.enter_context(nc.sbuf_tensor("y_s", [128, D], F32))
    mr2 = ph_f.enter_context(nc.sbuf_tensor("mr2", [128, 1], F32))
    nm2 = ph_f.enter_context(nc.sbuf_tensor("nm2", [128, 1], F32))
    ssq2 = ph_f.enter_context(nc.sbuf_tensor("ssq2", [128, 1], F32))
    sd2 = ph_f.enter_context(nc.sbuf_tensor("sd2", [128, 1], F32))
    rstd2 = ph_f.enter_context(nc.sbuf_tensor("rstd2", [128, 1], F32))
    b22 = ph_f.enter_context(nc.sbuf_tensor("b22", [128, 1], F32))
    sq2 = q_ax

    ph_fps = es.enter_context(contextlib.ExitStack())
    ps_gate = ph_fps.enter_context(nc.psum_tensor("ps_gate", [128, 512], F32))
    ps_up = ph_fps.enter_context(nc.psum_tensor("ps_up", [128, 512], F32))
    ps_dn = ph_fps.enter_context(nc.psum_tensor("ps_dn", [128, 1024], F32))

    n_scr0 = 16 * ((len(wq_jobs) + 1) // 2)
    n_scr1 = 16 * (len(wq_jobs) // 2)

    def emit_slab_loads(s, buf, pe_free=0, act_free=0, dve_free=0):
        ns = FFN_SLABS[s]
        wt(SP, c_dsw[0], n_scr0)
        wt(SP, c_dsw[1], n_scr1)
        if pe_free:
            wt(SP, c_pe, pe_free)
        if act_free:
            wt(SP, c_act, act_free)
        if dve_free:
            wt(SP, c_dve, dve_free)
        for hc in range(8):
            SP.append(lambda e, s=s, hc=hc, ns=ns, b=wgT[buf]: e.dma_start(
                b[:, hc * 512:hc * 512 + ns],
                d_sgq[s * 512:s * 512 + ns, hc * 128:(hc + 1) * 128],
                transpose=True))
            inc(SP, c_dst[buf], 16)
        for hc in range(8):
            SP.append(lambda e, s=s, hc=hc, ns=ns, b=wuT[buf]: e.dma_start(
                b[:, hc * 512:hc * 512 + ns],
                d_suq[s * 512:s * 512 + ns, hc * 128:(hc + 1) * 128],
                transpose=True))
            inc(SP, c_dst[buf], 16)
        for icc in range(ns // 128):
            SP.append(lambda e, s=s, icc=icc, b=wdT[buf]: e.dma_start(
                b[:, icc * 1024:(icc + 1) * 1024],
                d_sdq[:, s * 512 + icc * 128:s * 512 + (icc + 1) * 128],
                transpose=True))
            inc(SP, c_dst[buf], 16)
        return c_dst[buf].n

    slab_ready = [0] * len(FFN_SLABS)
    slab_ready[0] = emit_slab_loads(0, 0, pe_free=c_pe.n, act_free=c_act.n,
                                    dve_free=c_dve.n)
    slab_last_pe = [0] * len(FFN_SLABS)
    n_gu = 0
    if DEBUG_DUMP:
        for ch in range(FI // 128):
            SP.append(lambda e, ch=ch: e.dma_start(
                wfq_out[ch % 2][:], d_sgq[ch * 128:(ch + 1) * 128, :]))
            inc(SP, c_dbg, 16)
            wt(SP, c_dbg, c_dbg.n)
            SP.append(lambda e, ch=ch: e.dma_start(
                d_dbg_wgq[ch * 128:(ch + 1) * 128, :], wfq_out[ch % 2][:]))
            inc(SP, c_dbg, 16)
            wt(SP, c_dbg, c_dbg.n)
        wt(SP, c_dst[0], slab_ready[0])
        SP.append(lambda e: e.dma_start(d_dbg_wgt, wgT[0][:]))
        inc(SP, c_dbg, 16)
        SP.append(lambda e: e.dma_start(d_dbg_wdt, wdT[0][:]))
        inc(SP, c_dbg, 16)

    acc_fin = {}
    last_acc = 0
    last_silu = 0
    last_hmul = 0
    last_htr = 0
    last_dn = 0
    for s, ns in enumerate(FFN_SLABS):
        nic = ns // 128
        buf = s % 2
        if s + 1 < len(FFN_SLABS):
            slab_ready[s + 1] = emit_slab_loads(
                s + 1, (s + 1) % 2,
                pe_free=slab_last_pe[s - 1] if s >= 1 else 0)
        def emit_gu(tt):
            nonlocal n_gu
            wt(PE, c_dst[buf], slab_ready[s])
            if s == 0:
                wt(PE, c_dve, xqT_n[tt])
            if last_silu:
                wt(PE, c_act, last_silu)
            if last_hmul:
                wt(PE, c_dve, last_hmul)
            for wsel2, bank in ((wgT[buf], ps_gate), (wuT[buf], ps_up)):
                for ic in range(8):
                    PE.append(lambda e, w=wsel2, ic=ic, tt=tt, b=bank, ns=ns:
                              e.matmul(
                        b[:, 0:ns],
                        xqT[:, ic * T + tt * 128:ic * T + (tt + 1) * 128],
                        w[:, ic * 512:ic * 512 + ns],
                        start=(ic == 0), stop=(ic == 7)))
            n_gu = inc(PE, c_pe)

        emit_gu(0)
        for tt in range(NT):
            wt(SA, c_pe, n_gu)
            if last_hmul:
                wt(SA, c_dve, last_hmul)
            SA.append(lambda a, ns=ns: a.activation(
                gs_s[:, 0:ns], ps_gate[:, 0:ns], AF.Silu))
            last_silu = inc(SA, c_act)
            wt(DV, c_act, last_silu)
            DV.append(lambda v, ns=ns: v.tensor_tensor(
                h_s[:, 0:ns], gs_s[:, 0:ns], ps_up[:, 0:ns], ALU.mult))
            last_hmul = inc(DV, c_dve)
            if tt + 1 < NT:
                emit_gu(tt + 1)
            if last_htr:
                wt(DV, c_pe, last_htr)
            n_hq = emit_quant(h_s[:, 0:ns], hq_s[:, 0:ns], ns)
            wt(PE, c_dve, n_hq)
            for icc in range(nic):
                PE.append(lambda e, icc=icc: e.transpose(
                    ps_tr[:, icc * 128:(icc + 1) * 128],
                    hq_s[:, icc * 128:(icc + 1) * 128], id_bf[:]))
            last_htr = inc(PE, c_pe)
            wt(DV, c_pe, last_htr)
            if last_dn:
                wt(DV, c_pe, last_dn)
            DV.append(lambda v, nic=nic: v.tensor_copy(
                hqT_s[:, 0:nic * 128], ps_tr[:, 0:nic * 128]))
            n_hqT = inc(DV, c_dve)
            wt(PE, c_dve, n_hqT)
            if last_acc:
                wt(PE, c_dve, last_acc)
            for icc in range(nic):
                for nb in range(2):
                    PE.append(lambda e, icc=icc, nb=nb, b=wdT[buf], nic=nic:
                              e.matmul(
                        ps_dn[:, nb * 512:(nb + 1) * 512],
                        hqT_s[:, icc * 128:(icc + 1) * 128],
                        b[:, icc * 1024 + nb * 512:icc * 1024 + (nb + 1) * 512],
                        start=(icc == 0), stop=(icc == nic - 1)))
            last_dn = inc(PE, c_pe)
            slab_last_pe[s] = last_dn
            wt(DV, c_pe, last_dn)
            DV.append(lambda v, tt=tt: v.tensor_tensor(
                acc[:, tt * D:(tt + 1) * D], acc[:, tt * D:(tt + 1) * D],
                ps_dn[:], ALU.add))
            last_acc = inc(DV, c_dve)
            if s == len(FFN_SLABS) - 1:
                acc_fin[tt] = last_acc

    last_sq2 = 0
    for q in range(NT):
        wt(DV, c_dve, acc_fin[q])
        if q >= 1:
            wt(DV, c_act, last_sq2)
        src = acc[:, q * D:(q + 1) * D]
        DV.append(lambda v, s=src: v.tensor_reduce(mr2[:], s, axis=AX.X,
                                                   op=ALU.add))
        DV.append(lambda v: v.tensor_scalar(nm2[:], mr2[:], -1.0 / 1024.0,
                                            None, ALU.mult))
        n_nm = inc(DV, c_dve)
        wt(SA, c_dve, n_nm)
        SA.append(lambda a, s=src: a.activation(sq2[:], s, AF.Square,
                                                bias=nm2[:, 0:1], scale=1.0,
                                                accum_out=ssq2[:]))
        last_sq2 = inc(SA, c_act)
        SA.append(lambda a: a.activation(sd2[:], ssq2[:], AF.Ln,
                                         bias=lc_eps[:, 0:1], scale=1.0 / 1024.0))
        SA.append(lambda a: a.activation(rstd2[:], sd2[:], AF.Exp,
                                         bias=0.0, scale=-0.5))
        n_sd = inc(SA, c_act)
        wt(DV, c_act, n_sd)
        DV.append(lambda v: v.tensor_scalar(b22[:], mr2[:], rstd2[:, 0:1],
                                            -1.0 / 1024.0, ALU.mult, ALU.mult))
        n_rstd = inc(DV, c_dve)
        wt(SA, c_dve, n_rstd)
        if q >= 1:
            wt(SA, c_dout, 16 * q)
        SA.append(lambda a, s=src: a.activation(y_s[:], s, AF.Identity,
                                                bias=b22[:, 0:1],
                                                scale=rstd2[:, 0:1]))
        n_y = inc(SA, c_act)
        wt(SP, c_act, n_y)
        SP.append(lambda e, q=q: e.dma_start(
            d_out[q * 128:(q + 1) * 128, :], y_s[:]))
        inc(SP, c_dout, 16)

    wt(SP, c_dout, 16 * NT)

    with nc.Block() as blk:
        @blk.sync
        def _(e):
            for f in SP:
                f(e)

        @blk.gpsimd
        def _(e):
            for f in GP:
                f(e)

        @blk.tensor
        def _(e):
            for f in PE:
                f(e)

        @blk.scalar
        def _(e):
            for f in SA:
                f(e)

        @blk.vector
        def _(e):
            for f in DV:
                f(e)

    es.close()
    return nc


def kernel(L0_attn_x, L0_attn_Wq, L0_attn_bq, L0_attn_Wk, L0_attn_bk,
           L0_attn_Wv, L0_attn_bv, L0_attn_Wo, L0_attn_bo,
           L0_ffn_w_gate, L0_ffn_w_up, L0_ffn_w_down):
    x = np.ascontiguousarray(np.asarray(L0_attn_x, dtype=np.float32))
    wqT = np.ascontiguousarray(np.asarray(L0_attn_Wq, np.float32).T)
    wkT = np.ascontiguousarray(np.asarray(L0_attn_Wk, np.float32).T)
    wvT = np.ascontiguousarray(np.asarray(L0_attn_Wv, np.float32).T)
    woT = np.ascontiguousarray(np.asarray(L0_attn_Wo, np.float32).T)
    ident = np.eye(128, dtype=np.float32).astype(ml_dtypes.bfloat16)

    nc = build_program()

    lconsts = [build_lband_consts(0), build_lband_consts(1)]
    in_maps = []
    for c in range(8):
        b, half = c // 2, c % 2
        xb = x[b]
        if half == 0:
            x_rot = xb
        else:
            x_rot = np.ascontiguousarray(np.concatenate([xb[T:], xb[:T]], 0))
        m = {
            "xb": x_rot, "wqT": wqT, "wkT": wkT, "wvT": wvT, "woT": woT,
            "bq": np.ascontiguousarray(np.asarray(L0_attn_bq, np.float32)),
            "bk": np.ascontiguousarray(np.asarray(L0_attn_bk, np.float32)),
            "bv": np.ascontiguousarray(np.asarray(L0_attn_bv, np.float32)),
            "bo": np.ascontiguousarray(np.asarray(L0_attn_bo, np.float32)),
            "wg": np.ascontiguousarray(np.asarray(L0_ffn_w_gate, np.float32)),
            "wu": np.ascontiguousarray(np.asarray(L0_ffn_w_up, np.float32)),
            "wd": np.ascontiguousarray(np.asarray(L0_ffn_w_down, np.float32)),
            "ident": ident,
        }
        for k, v in lconsts[half].items():
            m["L_" + k] = v
        in_maps.append(m)

    res = run_bass_kernel_spmd(nc, in_maps, core_ids=list(range(8)))
    out = np.empty((B, S, D), np.float32)
    for c in range(8):
        b, half = c // 2, c % 2
        out[b, half * T:(half + 1) * T] = res.results[c]["out"]
    return out
